# version 02
# baseline (speedup 1.0000x reference)
import numpy as np
import concourse.bass as bass
import concourse.mybir as mybir
from concourse.bass_utils import run_bass_kernel_spmd

F32 = mybir.dt.float32
BF16 = mybir.dt.bfloat16
I32 = mybir.dt.int32
AF = mybir.ActivationFunctionType
ALU = mybir.AluOpType
AX = mybir.AxisListType

B, T, D = 4, 4096, 1024
H, FD, HD = 16, 16, 64
C = 128
NCH = T // C
F = 2048
E = 8
NAV = 128
EPS_LN = 1e-5
EPS_Z = 1e-10


class Sem:
    def __init__(self, h):
        self.h = h
        self.n = 0


def inc(ins, s, k=1):
    ins.then_inc(s.h, k)
    s.n += k
    return ins


def wge(eng, s, n=None):
    n = s.n if n is None else n
    assert n <= s.n, "wait on not-yet-emitted increment"
    if n > 0:
        eng.wait_ge(s.h, n)


class AD:
    def __init__(self, e):
        self._e = e

    def __getattr__(self, name):
        fn = getattr(self._e, name)
        if name in ("wait_ge", "drain") or not callable(fn):
            return fn
        def wrapped(*a, **k):
            self._e.drain()
            return fn(*a, **k)
        return wrapped


SM_GAMMA, SM_BETA, SM_BP1, SM_BP2 = 0, 16, 32, 160
SM_BO, SM_CM, SM_LG, SM_OH, SM_MX = 168, 1192, 1224, 1232, 1240
SM_L1, SM_G, SM_BT, SM_ONE1, SM_ONES = 1248, 1376, 1504, 2528, 2530
SMF_N = 2688
SB_OH, SB_BT = 0, 8
SMB_N = 1152

A_WQK, A_WV, A_WO = 0, 4096, 12288
A_XB, A_XT = 20480, 21504
A_XNQ, A_XNK, A_XNQT, A_XNKT = 22528, 23552, 24576, 25600
A_QF, A_KF, A_QFT = 26624, 30720, 34816
A_VSB, A_ATTM, A_OSB, A_OT, A_STB = 38912, 39952, 42000, 43024, 44048
A_W1E, A_W2E, A_HSB, A_X1T, A_RT, A_WO2 = 0, 16384, 32768, 40960, 45056, 49152
ARENA_N = 57344

AF_QK, AF_ZY, AF_ZZ, AF_X1R, AF_WP1, AF_XRT, AF_WP2 = 0, 512, 1024, 1664, 2688, 3712, 3728
AF_X2, AF_W2S = 0, 2048
ARENAF_N = 4096


def build_nc():
    nc = bass.Bass("TRN2", target_bir_lowering=False)
    d = {}
    for name, shape in [
        ("xp", [T, D]), ("xr", [T, D]), ("cmask", [1, NCH]),
        ("wqk", [D, 512]), ("wv", [D, D]), ("wo", [D, D]),
        ("gamma", [1, FD]), ("beta", [1, FD]),
        ("wp1", [D, NAV]), ("bp1", [1, NAV]), ("wp2", [NAV, E]), ("bp2", [1, E]),
        ("W1", [E, D, F]), ("W2", [E, F, D]),
        ("b1t", [E, 128, F // 128]), ("b2t", [E, 128, D // 128]),
        ("wo2", [D, D]), ("bo", [1, D]),
    ]:
        d[name] = nc.dram_tensor(name, shape, F32, kind="ExternalInput").ap()
    out_h = nc.dram_tensor("out", [T // 2, D], F32, kind="ExternalOutput").ap()

    from contextlib import ExitStack
    with ExitStack() as stack:
        en = stack.enter_context
        AR = en(nc.sbuf_tensor("arena", [128, ARENA_N], BF16))
        ARF = en(nc.sbuf_tensor("arf", [128, ARENAF_N], F32))
        xf = en(nc.sbuf_tensor("xf", [128, 2 * D], F32))
        cent = en(nc.sbuf_tensor("cent", [128, 1024], F32))
        rbuf = en(nc.sbuf_tensor("rbuf", [128, 1024], F32))
        stats = en(nc.sbuf_tensor("stats", [128, 16], F32))
        wstage = en(nc.sbuf_tensor("wstage", [128, 2 * 2048], F32))
        x1 = en(nc.sbuf_tensor("x1", [128, 16 * 1024], BF16))
        smf = en(nc.sbuf_tensor("smf", [1, SMF_N], F32))
        smb = en(nc.sbuf_tensor("smb", [1, SMB_N], BF16))
        ident = en(nc.sbuf_tensor("ident", [128, 128], BF16))
        identf = en(nc.sbuf_tensor("identf", [128, 128], F32))
        masku = en(nc.sbuf_tensor("masku", [128, 128], BF16))
        ones1 = en(nc.sbuf_tensor("ones1", [1, 128], BF16))
        ioti = en(nc.sbuf_tensor("ioti", [128, 128], I32))
        Ksa = en(nc.sbuf_tensor("Ksa", [128, 32], F32))
        Ksb = en(nc.sbuf_tensor("Ksb", [128, 32], BF16))
        onescol = en(nc.sbuf_tensor("onescol", [128, 1], BF16))
        gbr = en(nc.sbuf_tensor("gbr", [128, 32], F32))
        crep = en(nc.sbuf_tensor("crep", [128, 64], F32))
        mrep = en(nc.sbuf_tensor("mrep", [128, 8], F32))
        b1e = en(nc.sbuf_tensor("b1e", [128, 16], F32))
        b2e = en(nc.sbuf_tensor("b2e", [128, 8], F32))
        b2eb = en(nc.sbuf_tensor("b2eb", [128, 8], BF16))
        ps_state = en(nc.psum_tensor("ps_state", [128, 2048], F32))
        ps_m = en(nc.psum_tensor("ps_m", [128, 512], F32))
        ps_a = en(nc.psum_tensor("ps_a", [128, 512], F32))
        ps_b = en(nc.psum_tensor("ps_b", [128, 512], F32))
        ps_o = en(nc.psum_tensor("ps_o", [128, 512], F32))
        sems = {}
        for nm in ("sXa", "sXb", "sW0", "sW1", "sM", "sG0", "sG1",
                   "sO0", "sO1", "sT", "sS", "sV", "sI"):
            sems[nm] = Sem(en(nc.semaphore(nm)))
        blk = en(nc.Block())
        sXa, sXb, sW0, sW1, sM = (sems["sXa"], sems["sXb"], sems["sW0"],
                                  sems["sW1"], sems["sM"])
        sG0, sG1, sO0, sO1 = sems["sG0"], sems["sG1"], sems["sO0"], sems["sO1"]
        sT, sS, sV, sI = sems["sT"], sems["sS"], sems["sV"], sems["sI"]
        marks = {}
        started = set()

        def VEC(f):
            return blk.vector(lambda eng: f(AD(eng)))

        def SCA(f):
            return blk.scalar(lambda eng: f(AD(eng)))

        def a(off, n):
            return AR[:, off:off + n]

        def af(off, n):
            return ARF[:, off:off + n]

        @blk.gpsimd
        def _(eng):
            inc(eng.iota(ioti[:], pattern=[[1, 128]], base=0, channel_multiplier=-1), sI)

        @VEC
        def _(eng):
            wge(eng, sI)
            eng.tensor_scalar(ident[:], ioti[:], 0, None, ALU.is_equal)
            eng.tensor_scalar(identf[:], ioti[:], 0, None, ALU.is_equal)
            eng.tensor_scalar(masku[:], ioti[:], 0, None, ALU.is_ge)
            eng.memset(ones1[:], 1.0)
            eng.memset(smf[0:1, SM_ONE1:SM_ONE1 + 1], 1.0)
            eng.memset(smf[0:1, SM_ONES:SM_ONES + 128], 1.0)
            eng.memset(a(A_XNQ, 1024), 0.0)
            eng.memset(a(A_XNK, 1024), 0.0)
            eng.memset(Ksa[:], 0.0)
            eng.memset(onescol[:], 1.0)
            inc(eng.memset(a(A_VSB, 1040), 1.0), sV)

        @blk.sync
        def _(eng):
            inc(eng.dma_start(smf[0:1, SM_GAMMA:SM_GAMMA + 16], d["gamma"]), sM, 16)
            inc(eng.dma_start(smf[0:1, SM_BETA:SM_BETA + 16], d["beta"]), sM, 16)
            inc(eng.dma_start(smf[0:1, SM_BP1:SM_BP1 + 128], d["bp1"]), sM, 16)
            inc(eng.dma_start(smf[0:1, SM_BP2:SM_BP2 + 8], d["bp2"]), sM, 16)
            inc(eng.dma_start(smf[0:1, SM_BO:SM_BO + 1024], d["bo"]), sM, 16)
            inc(eng.dma_start(smf[0:1, SM_CM:SM_CM + 32], d["cmask"]), sM, 16)
            for i in range(8):
                inc(eng.dma_start(af(AF_WP1 + i * 128, 128),
                                  d["wp1"].rearrange("(a p) n -> a p n", p=128)[i]), sM, 16)
            inc(eng.dma_start(af(AF_WP2, 8), d["wp2"].rearrange("p n -> p n")), sM, 16)
        mark_small = sM.n

        @blk.tensor
        def _(eng):
            wge(eng, sM, mark_small)
            wge(eng, sV)
            onesf = smf[0:1, SM_ONES:SM_ONES + 128]
            eng.matmul(ps_a[:, 0:16], onesf, smf[0:1, SM_GAMMA:SM_GAMMA + 16], start=True, stop=True)
            eng.matmul(ps_a[:, 16:32], onesf, smf[0:1, SM_BETA:SM_BETA + 16], start=True, stop=True)
            inc(eng.matmul(ps_a[:, 32:64], onesf, smf[0:1, SM_CM:SM_CM + 32], start=True, stop=True), sT)

        @SCA
        def _(eng):
            wge(eng, sT)
            eng.copy(gbr[:, 0:32], ps_a[:, 0:32])
            eng.copy(crep[:, 0:32], ps_a[:, 32:64])
            inc(eng.memzero(crep[:, 32:64]), sS)

        @VEC
        def _(eng):
            wge(eng, sS)
            inc(eng.tensor_scalar(crep[:, 32:64], crep[:, 32:64], 1.0, None, ALU.add), sV)
        mark_rep = sV.n

        wload = ([("wqk", A_WQK, 512, i) for i in range(8)] +
                 [("wv", A_WV, 1024, i) for i in range(8)] +
                 [("wo", A_WO, 1024, i) for i in range(8)] +
                 [("wo2", A_WO2, 1024, i) for i in range(8)])
        for j, (nm, base, ncols, i) in enumerate(wload):
            bufo = (j % 2) * 2048
            sw = sW0 if j % 2 == 0 else sW1
            @blk.sync
            def _(eng, nm=nm, i=i, bufo=bufo, ncols=ncols, j=j):
                mk = marks.get(("wst", j % 2))
                if mk is not None:
                    wge(eng, sS, mk)
                inc(eng.dma_start(wstage[:, bufo:bufo + ncols],
                                  d[nm].rearrange("(a p) n -> a p n", p=128)[i]), sw, 16)
            @SCA
            def _(eng, base=base, i=i, bufo=bufo, ncols=ncols, sw=sw):
                wge(eng, sw)
                inc(eng.copy(a(base + i * ncols, ncols), wstage[:, bufo:bufo + ncols]), sS)
            marks[("wst", j % 2)] = sS.n
        mark_p0cast = sS.n

        def kchunk(xin, ch, do_q, do_copies, mask_off, x1_dest, phase):
            bi = ch % 2
            sL = sXa if bi == 0 else sXb
            xfv = xf[:, bi * D:(bi + 1) * D]
            xb = a(A_XB, 1024)
            xt = a(A_XT, 1024)

            @blk.sync
            def _(eng):
                mk = marks.get(("xf", bi))
                if mk is not None:
                    wge(eng, mk[0], mk[1])
                inc(eng.dma_start(xfv, xin[ch * 128:(ch + 1) * 128, :]), sL, 16)

            @SCA
            def _(eng):
                wge(eng, sL)
                wge(eng, sT)
                inc(eng.copy(xb, xfv), sS)

            @blk.tensor
            def _(eng):
                wge(eng, sS)
                wge(eng, sV)
                for k in range(8):
                    ps = ps_a if k < 4 else ps_b
                    ins = eng.matmul(ps[:, (k % 4) * 128:(k % 4 + 1) * 128],
                                     xb[:, k * 128:(k + 1) * 128], ident[:],
                                     start=True, stop=True)
                inc(ins, sT)

            @SCA
            def _(eng):
                wge(eng, sT)
                eng.copy(xt[:, 0:512], ps_a[:])
                inc(eng.copy(xt[:, 512:1024], ps_b[:]), sS)

            @blk.tensor
            def _(eng):
                wge(eng, sS)
                for dsl in range(8):
                    eng.matmul(ps_a[:], xt[:, dsl * 128:(dsl + 1) * 128],
                               a(A_WQK + dsl * 512, 512), start=(dsl == 0), stop=(dsl == 7))
                for dsl in range(8):
                    eng.matmul(ps_b[:], xt[:, dsl * 128:(dsl + 1) * 128],
                               a(A_WV + dsl * 1024, 512), start=(dsl == 0), stop=(dsl == 7))
                for dsl in range(8):
                    ins = eng.matmul(ps_o[:], xt[:, dsl * 128:(dsl + 1) * 128],
                                     a(A_WV + dsl * 1024 + 512, 512), start=(dsl == 0), stop=(dsl == 7))
                inc(ins, sT)

            @SCA
            def _(eng):
                wge(eng, sT)
                eng.copy(af(AF_QK, 512), ps_a[:])
                v3 = a(A_VSB, 1040).rearrange("p (h c) -> p h c", c=65)
                eng.copy(v3[:, 0:8, 0:64], ps_b[:].rearrange("p (h c) -> p h c", c=64))
                inc(eng.copy(v3[:, 8:16, 0:64], ps_o[:].rearrange("p (h c) -> p h c", c=64)), sS)
            if not do_q:
                marks[("xf", bi)] = (sS, sS.n)

            zzv = af(AF_ZZ, 640)
            @VEC
            def _(eng):
                wge(eng, sS)
                q3 = af(AF_QK, 512).rearrange("p (g f) -> p g f", f=16)
                mu = zzv[:, 512:544].rearrange("p (g o) -> p g o", o=1)
                eng.reduce_sum(mu, q3, axis=AX.X)
                eng.tensor_scalar(zzv[:, 512:544], zzv[:, 512:544], 1.0 / 16, None, ALU.mult)
                c3 = cent[:, 0:512].rearrange("p (g f) -> p g f", f=16)
                eng.tensor_tensor(c3, q3, mu.broadcast_to([128, 32, 16]), ALU.subtract)
                s3 = cent[:, 512:1024].rearrange("p (g f) -> p g f", f=16)
                eng.tensor_tensor(s3, c3, c3, ALU.mult)
                vs = zzv[:, 544:576].rearrange("p (g o) -> p g o", o=1)
                eng.reduce_sum(vs, s3, axis=AX.X)
                inc(eng.tensor_scalar(zzv[:, 544:576], zzv[:, 544:576],
                                      1.0 / 16, EPS_LN, ALU.mult, ALU.add), sV)

            @SCA
            def _(eng):
                wge(eng, sV)
                inc(eng.activation(zzv[:, 608:640], zzv[:, 544:576], AF.Sqrt), sS)

            @VEC
            def _(eng):
                wge(eng, sS)
                eng.reciprocal(zzv[:, 576:608], zzv[:, 608:640])
                c3 = cent[:, 0:512].rearrange("p (g f) -> p g f", f=16)
                rs = zzv[:, 576:608].rearrange("p (g o) -> p g o", o=1)
                eng.tensor_tensor(c3, c3, rs.broadcast_to([128, 32, 16]), ALU.mult)
                gam = gbr[:, 0:16].unsqueeze(1).broadcast_to([128, 16, 16])
                bet = gbr[:, 16:32].unsqueeze(1).broadcast_to([128, 16, 16])
                xk3 = a(A_XNK, 1024).rearrange("p (h f) -> p h f", f=64)[:, :, 0:16]
                ck = cent[:, 256:512].rearrange("p (h f) -> p h f", f=16)
                eng.tensor_tensor(xk3, ck, gam, ALU.mult)
                eng.tensor_tensor(xk3, xk3, bet, ALU.add)
                mk = crep[:, mask_off + ch:mask_off + ch + 1]
                eng.tensor_scalar(xk3, xk3, mk, None, ALU.mult)
                kk = a(A_XNK, 1024).rearrange("p (h f) -> p h f", f=64)[:, :, 0:16]
                eng.tensor_tensor(a(A_KF, 4096).rearrange("p (h i j) -> p h i j", i=16, j=16),
                                  kk.unsqueeze(3).broadcast_to([128, 16, 16, 16]),
                                  kk.unsqueeze(2).broadcast_to([128, 16, 16, 16]), ALU.mult)
                if do_q:
                    xq3 = a(A_XNQ, 1024).rearrange("p (h f) -> p h f", f=64)[:, :, 0:16]
                    cq = cent[:, 0:256].rearrange("p (h f) -> p h f", f=16)
                    eng.tensor_tensor(xq3, cq, gam, ALU.mult)
                    eng.tensor_tensor(xq3, xq3, bet, ALU.add)
                    qq = a(A_XNQ, 1024).rearrange("p (h f) -> p h f", f=64)[:, :, 0:16]
                    eng.tensor_tensor(a(A_QF, 4096).rearrange("p (h i j) -> p h i j", i=16, j=16),
                                      qq.unsqueeze(3).broadcast_to([128, 16, 16, 16]),
                                      qq.unsqueeze(2).broadcast_to([128, 16, 16, 16]), ALU.mult)
                inc(eng.engine_nop(), sV)

            if do_copies:
                @SCA
                def _(eng):
                    wge(eng, sT)
                    wge(eng, sV)
                    eng.activation(a(A_STB, 2048), ps_state[:], AF.Copy, bias=0.0, scale=1.0 / 16)
                    inc(eng.activation(Ksb[:], Ksa[:], AF.Copy, bias=0.0, scale=1.0 / 16), sS)
            mark_copies = sS.n

            if do_q:
                for src, dst in ((A_XNQ, A_XNQT), (A_XNK, A_XNKT)):
                    @blk.tensor
                    def _(eng, src=src):
                        wge(eng, sS)
                        wge(eng, sV)
                        for k in range(8):
                            ps = ps_a if k < 4 else ps_b
                            ins = eng.matmul(ps[:, (k % 4) * 128:(k % 4 + 1) * 128],
                                             a(src + k * 128, 128), ident[:], start=True, stop=True)
                        inc(ins, sT)
                    @SCA
                    def _(eng, dst=dst):
                        wge(eng, sT)
                        eng.copy(a(dst, 512), ps_a[:])
                        inc(eng.copy(a(dst + 512, 512), ps_b[:]), sS)
                for wv_ in range(4):
                    @blk.tensor
                    def _(eng, wv_=wv_):
                        wge(eng, sS)
                        for k in range(8):
                            ps = ps_a if k < 4 else ps_b
                            ins = eng.matmul(ps[:, (k % 4) * 128:(k % 4 + 1) * 128],
                                             a(A_QF + wv_ * 1024 + k * 128, 128), ident[:],
                                             start=True, stop=True)
                        inc(ins, sT)
                    @SCA
                    def _(eng, wv_=wv_):
                        wge(eng, sT)
                        eng.copy(a(A_QFT + wv_ * 1024, 512), ps_a[:])
                        inc(eng.copy(a(A_QFT + wv_ * 1024 + 512, 512), ps_b[:]), sS)

            @blk.tensor
            def _(eng):
                if do_copies:
                    wge(eng, sS, mark_copies)
                wge(eng, sV)
                for h in range(16):
                    for kc in range(2):
                        col = (h * 2 + kc) * 64
                        key = (phase, "st", col // 512)
                        st = key not in started
                        started.add(key)
                        eng.matmul(ps_state[:, col:col + 64],
                                   a(A_KF + h * 256 + kc * 128, 128),
                                   a(A_VSB + h * 65, 64), start=st, stop=False,
                                   skip_group_check=True)
                ins = None
                for h in range(16):
                    for kc in range(2):
                        col = h * 2 + kc
                        ins = eng.matmul(ps_m[:, col:col + 1],
                                         a(A_KF + h * 256 + kc * 128, 128),
                                         onescol[:], start=True, stop=True)
                inc(ins, sT)

            @VEC
            def _(eng):
                wge(eng, sT)
                inc(eng.tensor_tensor(Ksa[:], Ksa[:], ps_m[:, 0:32], ALU.add), sV)

            if not do_q:
                return

            xnqT = a(A_XNQT, 1024)
            xnkT = a(A_XNKT, 1024)
            for key in (("attev", 0), ("attev", 1), ("oev", 0), ("oev", 1)):
                marks.pop(key, None)
            for h in range(16):
                bank = ps_a if h % 2 == 0 else ps_b
                @blk.tensor
                def _(eng, h=h, bank=bank):
                    mk = marks.get(("attev", h % 2))
                    if mk is not None:
                        wge(eng, sS, mk)
                    else:
                        wge(eng, sS)
                    off = 64 * (h % 2)
                    tl = (h // 2) * 128
                    inc(eng.matmul(bank[:, 0:128], xnkT[off:off + 64, tl:tl + 128],
                                   xnqT[off:off + 64, tl:tl + 128], start=True, stop=True), sT)
                @SCA
                def _(eng, h=h, bank=bank):
                    wge(eng, sT)
                    inc(eng.activation(a(A_ATTM + h * 128, 128), bank[:, 0:128],
                                       AF.Square, bias=0.0, scale=0.25), sS)
                marks[("attev", h % 2)] = sS.n
                @VEC
                def _(eng, h=h):
                    wge(eng, sS)
                    inc(eng.tensor_tensor(a(A_ATTM + h * 128, 128), a(A_ATTM + h * 128, 128),
                                          masku[:], ALU.mult), sV)

            for h in range(16):
                bank = ps_a if h % 2 == 0 else ps_b
                @blk.tensor
                def _(eng, h=h, bank=bank):
                    mk = marks.get(("oev", h % 2))
                    if mk is not None:
                        wge(eng, sS, mk)
                    wge(eng, sV)
                    eng.matmul(bank[:, 0:65], a(A_ATTM + h * 128, 128),
                               a(A_VSB + h * 65, 65), start=True, stop=False)
                    for kc in range(2):
                        eng.matmul(bank[:, 0:64], a(A_QFT + h * 256 + kc * 128, 128),
                                   a(A_STB + (h * 2 + kc) * 64, 64),
                                   start=False, stop=False)
                    for kc in range(2):
                        ins = eng.matmul(bank[:, 64:65], a(A_QFT + h * 256 + kc * 128, 128),
                                         Ksb[:, h * 2 + kc:h * 2 + kc + 1],
                                         start=False, stop=(kc == 1))
                    inc(ins, sT)
                @SCA
                def _(eng, h=h, bank=bank):
                    wge(eng, sT)
                    eng.copy(a(A_OSB + h * 64, 64), bank[:, 0:64])
                    inc(eng.copy(af(AF_ZY + h, 1), bank[:, 64:65]), sS)
                marks[("oev", h % 2)] = sS.n

            @VEC
            def _(eng):
                wge(eng, sS)
                eng.tensor_scalar(zzv[:, 16:32], af(AF_ZY, 16), EPS_Z, None, ALU.add)
                eng.reciprocal(zzv[:, 32:48], zzv[:, 16:32])
                o3 = a(A_OSB, 1024).rearrange("p (h c) -> p h c", c=64)
                zr = zzv[:, 32:48].rearrange("p (h o) -> p h o", o=1)
                inc(eng.tensor_tensor(o3, o3, zr.broadcast_to([128, 16, 64]), ALU.mult), sV)

            @blk.tensor
            def _(eng):
                wge(eng, sV)
                wge(eng, sS)
                for k in range(8):
                    ps = ps_a if k < 4 else ps_b
                    ins = eng.matmul(ps[:, (k % 4) * 128:(k % 4 + 1) * 128],
                                     a(A_OSB + k * 128, 128), ident[:], start=True, stop=True)
                inc(ins, sT)
            @SCA
            def _(eng):
                wge(eng, sT)
                eng.copy(a(A_OT, 512), ps_a[:])
                inc(eng.copy(a(A_OT + 512, 512), ps_b[:]), sS)
            @blk.tensor
            def _(eng):
                wge(eng, sS)
                for bk in range(8):
                    for w in range(2):
                        ps = ps_a if w == 0 else ps_b
                        ins = eng.matmul(ps[:], a(A_OT + bk * 128, 128),
                                         a(A_WO + bk * 1024 + w * 512, 512),
                                         start=(bk == 0), stop=(bk == 7))
                inc(ins, sT)

            @VEC
            def _(eng):
                wge(eng, sT)
                eng.tensor_tensor(rbuf[:, 0:512], ps_a[:], xfv[:, 0:512], ALU.add)
                eng.tensor_tensor(rbuf[:, 512:1024], ps_b[:], xfv[:, 512:1024], ALU.add)
                eng.reduce_sum(stats[:, 0:1], rbuf[:], axis=AX.X)
                eng.tensor_tensor(cent[:], rbuf[:], rbuf[:], ALU.mult)
                eng.reduce_sum(stats[:, 1:2], cent[:], axis=AX.X)
                eng.tensor_scalar(stats[:, 2:3], stats[:, 0:1], 1.0 / D, None, ALU.mult)
                eng.tensor_tensor(stats[:, 3:4], stats[:, 2:3], stats[:, 2:3], ALU.mult)
                eng.scalar_tensor_tensor(stats[:, 4:5], stats[:, 1:2], 1.0 / D, stats[:, 3:4],
                                         ALU.mult, ALU.subtract)
                inc(eng.tensor_scalar(stats[:, 4:5], stats[:, 4:5], EPS_LN, None, ALU.add), sV)
            @SCA
            def _(eng):
                wge(eng, sV)
                inc(eng.activation(stats[:, 7:8], stats[:, 4:5], AF.Sqrt), sS)
            @VEC
            def _(eng):
                wge(eng, sS)
                eng.reciprocal(stats[:, 5:6], stats[:, 7:8])
                eng.tensor_tensor(stats[:, 6:7], stats[:, 2:3], stats[:, 5:6], ALU.mult)
                eng.tensor_scalar(stats[:, 6:7], stats[:, 6:7], -1.0, None, ALU.mult)
                dst, dco = x1_dest
                inc(eng.scalar_tensor_tensor(dst[:, dco:dco + 1024], rbuf[:], stats[:, 5:6],
                                             stats[:, 6:7].broadcast_to([128, 1024]),
                                             ALU.mult, ALU.add), sV)
            marks[("xf", bi)] = (sV, sV.n)

        x1r = af(AF_X1R, 1024)
        for ch in range(NCH):
            kchunk(d["xr"], ch, do_q=(ch == NCH - 1), do_copies=(ch == NCH - 1),
                   mask_off=32, x1_dest=(ARF, AF_X1R), phase=1)

        xrt = af(AF_XRT, 16)
        @blk.tensor
        def _(eng):
            wge(eng, sV)
            wge(eng, sS)
            for k in range(8):
                ps = ps_a if k < 4 else ps_b
                ins = eng.matmul(ps[:, (k % 4) * 128:(k % 4 + 1) * 128],
                                 x1r[:, k * 128:(k + 1) * 128], identf[:], start=True, stop=True)
            inc(ins, sT)
        @SCA
        def _(eng):
            wge(eng, sT)
            ins = None
            for k in range(8):
                ps = ps_a if k < 4 else ps_b
                ins = eng.copy(xrt[:, k:k + 1], ps[:, (k % 4) * 128 + 127:(k % 4) * 128 + 128])
            inc(ins, sS)
        @blk.tensor
        def _(eng):
            wge(eng, sS)
            for k in range(8):
                ins = eng.matmul(ps_o[0:1, 0:128], xrt[:, k:k + 1],
                                 af(AF_WP1 + k * 128, 128), start=(k == 0), stop=(k == 7))
            inc(ins, sT)
        @VEC
        def _(eng):
            wge(eng, sT)
            inc(eng.tensor_tensor(smf[0:1, SM_L1:SM_L1 + 128], ps_o[0:1, 0:128],
                                  smf[0:1, SM_BP1:SM_BP1 + 128], ALU.add), sV)
        @SCA
        def _(eng):
            wge(eng, sV)
            inc(eng.activation(smf[0:1, SM_G:SM_G + 128], smf[0:1, SM_L1:SM_L1 + 128], AF.Gelu), sS)
        @blk.tensor
        def _(eng):
            wge(eng, sS)
            inc(eng.matmul(ps_o[:, 128:129], smf[0:1, SM_G:SM_G + 128],
                           smf[0:1, SM_ONE1:SM_ONE1 + 1], start=True, stop=True), sT)
        @SCA
        def _(eng):
            wge(eng, sT)
            inc(eng.copy(xrt[:, 8:9], ps_o[:, 128:129]), sS)
        @blk.tensor
        def _(eng):
            wge(eng, sS)
            inc(eng.matmul(ps_o[0:1, 136:144], xrt[:, 8:9], af(AF_WP2, 8), start=True, stop=True), sT)
        @VEC
        def _(eng):
            wge(eng, sT)
            eng.tensor_tensor(smf[0:1, SM_LG:SM_LG + 8], ps_o[0:1, 136:144],
                              smf[0:1, SM_BP2:SM_BP2 + 8], ALU.add)
            eng.reduce_max(smf[0:1, SM_MX:SM_MX + 1], smf[0:1, SM_LG:SM_LG + 8], axis=AX.X)
            inc(eng.tensor_tensor(smf[0:1, SM_OH:SM_OH + 8], smf[0:1, SM_LG:SM_LG + 8],
                                  smf[0:1, SM_MX:SM_MX + 1].broadcast_to([1, 8]), ALU.is_equal), sV)
        @SCA
        def _(eng):
            wge(eng, sV)
            inc(eng.copy(smb[0:1, SB_OH:SB_OH + 8], smf[0:1, SM_OH:SM_OH + 8]), sS)
        @blk.tensor
        def _(eng):
            wge(eng, sS)
            inc(eng.matmul(ps_o[:, 144:152], ones1[:], smb[0:1, SB_OH:SB_OH + 8],
                           start=True, stop=True), sT)
        @SCA
        def _(eng):
            wge(eng, sT)
            inc(eng.copy(mrep[:], ps_o[:, 144:152]), sS)
        mark_mrep = sS.n

        @blk.sync
        def _(eng):
            wge(eng, sT)
            wge(eng, sV)
            inc(eng.dma_start(af(AF_W2S, 128).rearrange("p (e f) -> p e f", e=8),
                              d["b1t"].rearrange("e p f -> p e f")), sM, 16)
            inc(eng.dma_start(af(AF_W2S + 128, 64).rearrange("p (e f) -> p e f", e=8),
                              d["b2t"].rearrange("e p f -> p e f")), sM, 16)
        @VEC
        def _(eng):
            wge(eng, sM)
            wge(eng, sS, mark_mrep)
            b13 = af(AF_W2S, 128).rearrange("p (e f) -> p e f", e=8)
            b23 = af(AF_W2S + 128, 64).rearrange("p (e f) -> p e f", e=8)
            for e in range(E):
                if e == 0:
                    eng.tensor_scalar(b1e[:], b13[:, 0, :], mrep[:, 0:1], None, ALU.mult)
                    eng.tensor_scalar(b2e[:], b23[:, 0, :], mrep[:, 0:1], None, ALU.mult)
                else:
                    eng.scalar_tensor_tensor(b1e[:], b13[:, e, :], mrep[:, e:e + 1], b1e[:],
                                             ALU.mult, ALU.add)
                    eng.scalar_tensor_tensor(b2e[:], b23[:, e, :], mrep[:, e:e + 1], b2e[:],
                                             ALU.mult, ALU.add)
            inc(eng.engine_nop(), sV)
        @SCA
        def _(eng):
            wge(eng, sV)
            inc(eng.copy(b2eb[:], b2e[:]), sS)
        @blk.tensor
        def _(eng):
            wge(eng, sS)
            for w in range(2):
                for dsl in range(8):
                    ins = eng.matmul((ps_a if w == 0 else ps_b)[0:1, 0:512],
                                     b2eb[:, dsl:dsl + 1],
                                     a(A_WO2 + dsl * 1024 + w * 512, 512),
                                     start=(dsl == 0), stop=(dsl == 7))
            inc(ins, sT)
        mark_router_t = sT.n
        @VEC
        def _(eng):
            wge(eng, sT)
            eng.tensor_tensor(smf[0:1, SM_BT:SM_BT + 512], ps_a[0:1, 0:512],
                              smf[0:1, SM_BO:SM_BO + 512], ALU.add)
            inc(eng.tensor_tensor(smf[0:1, SM_BT + 512:SM_BT + 1024], ps_b[0:1, 0:512],
                                  smf[0:1, SM_BO + 512:SM_BO + 1024], ALU.add), sV)
        @SCA
        def _(eng):
            wge(eng, sV)
            inc(eng.copy(smb[0:1, SB_BT:SB_BT + 1024], smf[0:1, SM_BT:SM_BT + 1024]), sS)

        @VEC
        def _(eng):
            inc(eng.memset(Ksa[:], 0.0), sV)

        for ch in range(NCH):
            kchunk(d["xp"], ch, do_q=(ch >= 16), do_copies=(ch >= 16),
                   mask_off=0, x1_dest=(x1, (ch - 16) * 1024), phase=2)

        mark_w1 = {}
        for dsl in range(8):
            for e in range(E):
                j = dsl * 8 + e
                bufo = (j % 2) * 2048
                sw = sW0 if j % 2 == 0 else sW1
                @blk.sync
                def _(eng, dsl=dsl, e=e, bufo=bufo, j=j, sw=sw):
                    mk = marks.get(("w1s", j % 2))
                    if mk is not None:
                        wge(eng, sV, mk)
                    else:
                        wge(eng, sS, mark_p0cast)
                    inc(eng.dma_start(wstage[:, bufo:bufo + 2048],
                                      d["W1"][e].rearrange("(a p) f -> a p f", p=128)[dsl]), sw, 16)
                @VEC
                def _(eng, dsl=dsl, e=e, bufo=bufo, sw=sw):
                    wge(eng, sw)
                    if e == 0:
                        wge(eng, sS, mark_mrep)
                        ins = eng.tensor_scalar(a(A_W1E + dsl * 2048, 2048),
                                                wstage[:, bufo:bufo + 2048],
                                                mrep[:, 0:1], None, ALU.mult)
                    else:
                        ins = eng.scalar_tensor_tensor(a(A_W1E + dsl * 2048, 2048),
                                                       wstage[:, bufo:bufo + 2048],
                                                       mrep[:, e:e + 1], a(A_W1E + dsl * 2048, 2048),
                                                       ALU.mult, ALU.add)
                    inc(ins, sV)
                marks[("w1s", j % 2)] = sV.n
        mark_w1done = sV.n

        for fsl in range(16):
            for e in range(E):
                j = fsl * 8 + e
                bufo = (j % 2) * 1024
                sg = sG0 if j % 2 == 0 else sG1
                @blk.gpsimd
                def _(eng, fsl=fsl, e=e, bufo=bufo, j=j, sg=sg):
                    mk = marks.get(("w2s", j % 2))
                    if mk is not None:
                        wge(eng, sV, mk)
                    else:
                        wge(eng, sT, mark_router_t)
                    inc(eng.dma_start(af(AF_W2S + bufo, 1024),
                                      d["W2"][e].rearrange("(a p) f -> a p f", p=128)[fsl]), sg, 16)
                @VEC
                def _(eng, fsl=fsl, e=e, bufo=bufo, sg=sg):
                    wge(eng, sg)
                    if e == 0:
                        ins = eng.tensor_scalar(a(A_W2E + fsl * 1024, 1024),
                                                af(AF_W2S + bufo, 1024),
                                                mrep[:, 0:1], None, ALU.mult)
                    else:
                        ins = eng.scalar_tensor_tensor(a(A_W2E + fsl * 1024, 1024),
                                                       af(AF_W2S + bufo, 1024),
                                                       mrep[:, e:e + 1], a(A_W2E + fsl * 1024, 1024),
                                                       ALU.mult, ALU.add)
                    inc(ins, sV)
                marks[("w2s", j % 2)] = sV.n
        mark_w2done = sV.n

        for stile in range(4):
            for key in (("hev", 0), ("hev", 1), ("rev", 0), ("rev", 1)):
                marks.pop(key, None)
            for tc in range(4):
                slot = stile * 4 + tc
                @blk.tensor
                def _(eng, slot=slot, tc=tc, stile=stile):
                    wge(eng, sS)
                    if stile == 0 and tc == 0:
                        wge(eng, sV, mark_w1done)
                    for k in range(8):
                        ps = ps_a if k < 4 else ps_b
                        ins = eng.matmul(ps[:, (k % 4) * 128:(k % 4 + 1) * 128],
                                         x1[:, slot * 1024 + k * 128: slot * 1024 + (k + 1) * 128],
                                         ident[:], start=True, stop=True)
                    inc(ins, sT)
                @SCA
                def _(eng, tc=tc):
                    wge(eng, sT)
                    x1t3 = a(A_X1T, 4096).rearrange("p (s t) -> p s t", t=512)
                    eng.copy(x1t3[:, 0:4, tc * 128:(tc + 1) * 128],
                             ps_a[:].rearrange("p (s t) -> p s t", t=128))
                    inc(eng.copy(x1t3[:, 4:8, tc * 128:(tc + 1) * 128],
                                 ps_b[:].rearrange("p (s t) -> p s t", t=128)), sS)

            for ft in range(16):
                bank = ps_a if ft % 2 == 0 else ps_b
                @blk.tensor
                def _(eng, ft=ft, bank=bank):
                    mk = marks.get(("hev", ft % 2))
                    if mk is not None:
                        wge(eng, sS, mk)
                    else:
                        wge(eng, sS)
                    for dsl in range(8):
                        ins = eng.matmul(bank[:], a(A_W1E + dsl * 2048 + ft * 128, 128),
                                         a(A_X1T + dsl * 512, 512), start=(dsl == 0), stop=(dsl == 7))
                    inc(ins, sT)
                @SCA
                def _(eng, ft=ft, bank=bank):
                    wge(eng, sT)
                    inc(eng.activation(a(A_HSB + ft * 512, 512), bank[:], AF.Gelu,
                                       bias=b1e[:, ft:ft + 1], scale=1.0), sS)
                marks[("hev", ft % 2)] = sS.n

            for dt in range(8):
                bank = ps_o if dt % 2 == 0 else ps_m
                @blk.tensor
                def _(eng, dt=dt, bank=bank):
                    mk = marks.get(("rev", dt % 2))
                    if mk is not None:
                        wge(eng, sS, mk)
                    else:
                        wge(eng, sS)
                    if dt == 0:
                        wge(eng, sV, mark_w2done)
                    for fsl in range(16):
                        ins = eng.matmul(bank[:], a(A_W2E + fsl * 1024 + dt * 128, 128),
                                         a(A_HSB + fsl * 512, 512), start=(fsl == 0), stop=(fsl == 15))
                    inc(ins, sT)
                @SCA
                def _(eng, dt=dt, bank=bank):
                    wge(eng, sT)
                    inc(eng.copy(a(A_RT + dt * 512, 512), bank[:]), sS)
                marks[("rev", dt % 2)] = sS.n

            for tc in range(4):
                slot = stile * 4 + tc
                pso0 = ps_state[:, (tc % 2) * 1024:(tc % 2) * 1024 + 512]
                pso1 = ps_state[:, (tc % 2) * 1024 + 512:(tc % 2) * 1024 + 1024]
                @blk.tensor
                def _(eng, tc=tc, pso0=pso0, pso1=pso1):
                    mk = marks.get(("o2", tc % 2))
                    if mk is not None:
                        wge(eng, sV, mk)
                    wge(eng, sS)
                    for w, pw in ((0, pso0), (1, pso1)):
                        for dsl in range(8):
                            eng.matmul(pw, a(A_RT + dsl * 512 + tc * 128, 128),
                                       a(A_WO2 + dsl * 1024 + w * 512, 512),
                                       start=(dsl == 0), stop=False)
                        ins = eng.matmul(pw, ones1[:], smb[0:1, SB_BT + w * 512:SB_BT + (w + 1) * 512],
                                         start=False, stop=True)
                    inc(ins, sT)
                @VEC
                def _(eng, slot=slot, pso0=pso0, pso1=pso1):
                    wge(eng, sT)
                    eng.scalar_tensor_tensor(rbuf[:, 0:512], pso0, 0.15,
                                             x1[:, slot * 1024:slot * 1024 + 512], ALU.mult, ALU.add)
                    eng.scalar_tensor_tensor(rbuf[:, 512:1024], pso1, 0.15,
                                             x1[:, slot * 1024 + 512:slot * 1024 + 1024],
                                             ALU.mult, ALU.add)
                    eng.reduce_sum(stats[:, 0:1], rbuf[:], axis=AX.X)
                    eng.tensor_tensor(cent[:], rbuf[:], rbuf[:], ALU.mult)
                    eng.reduce_sum(stats[:, 1:2], cent[:], axis=AX.X)
                    eng.tensor_scalar(stats[:, 2:3], stats[:, 0:1], 1.0 / D, None, ALU.mult)
                    eng.tensor_tensor(stats[:, 3:4], stats[:, 2:3], stats[:, 2:3], ALU.mult)
                    eng.scalar_tensor_tensor(stats[:, 4:5], stats[:, 1:2], 1.0 / D, stats[:, 3:4],
                                             ALU.mult, ALU.subtract)
                    inc(eng.tensor_scalar(stats[:, 4:5], stats[:, 4:5], EPS_LN / 4, None, ALU.add), sV)
                @SCA
                def _(eng):
                    wge(eng, sV)
                    inc(eng.activation(stats[:, 7:8], stats[:, 4:5], AF.Sqrt), sS)
                so = sO0 if slot % 2 == 0 else sO1
                @VEC
                def _(eng, slot=slot, so=so):
                    wge(eng, sS)
                    mko = marks.get(("x2", slot % 2))
                    if mko is not None:
                        wge(eng, so, mko)
                    eng.reciprocal(stats[:, 5:6], stats[:, 7:8])
                    eng.tensor_tensor(stats[:, 6:7], stats[:, 2:3], stats[:, 5:6], ALU.mult)
                    eng.tensor_scalar(stats[:, 6:7], stats[:, 6:7], -1.0, None, ALU.mult)
                    inc(eng.scalar_tensor_tensor(af(AF_X2 + (slot % 2) * 1024, 1024), rbuf[:],
                                                 stats[:, 5:6],
                                                 stats[:, 6:7].broadcast_to([128, 1024]),
                                                 ALU.mult, ALU.add), sV)
                marks[("o2", tc % 2)] = sV.n
                @blk.gpsimd
                def _(eng, slot=slot, so=so):
                    wge(eng, sV)
                    inc(eng.dma_start(out_h[slot * 128:(slot + 1) * 128, :],
                                      af(AF_X2 + (slot % 2) * 1024, 1024)), so, 16)
                marks[("x2", slot % 2)] = so.n

        @blk.gpsimd
        def _(eng):
            wge(eng, sO0, 8 * 16)
            wge(eng, sO1, 8 * 16)

    return nc


_NC_CACHE = []


def kernel(**inputs):
    X = np.asarray(inputs["L0_attn_hidden_states"], dtype=np.float32)
    Wq = np.asarray(inputs["L0_attn_Wq"], dtype=np.float32)
    Wk = np.asarray(inputs["L0_attn_Wk"], dtype=np.float32)
    Wv = np.asarray(inputs["L0_attn_Wv"], dtype=np.float32)
    Wo = np.asarray(inputs["L0_attn_Wo"], dtype=np.float32)
    gamma = np.asarray(inputs["L0_attn_gamma"], dtype=np.float32)
    beta = np.asarray(inputs["L0_attn_beta"], dtype=np.float32)
    Wp1 = np.asarray(inputs["L0_ffn_Wp1"], dtype=np.float32)
    bp1 = np.asarray(inputs["L0_ffn_bp1"], dtype=np.float32)
    Wp2 = np.asarray(inputs["L0_ffn_Wp2"], dtype=np.float32)
    bp2 = np.asarray(inputs["L0_ffn_bp2"], dtype=np.float32)
    W1 = np.asarray(inputs["L0_ffn_W1"], dtype=np.float32)
    b1 = np.asarray(inputs["L0_ffn_b1"], dtype=np.float32)
    W2 = np.asarray(inputs["L0_ffn_W2"], dtype=np.float32)
    b2 = np.asarray(inputs["L0_ffn_b2"], dtype=np.float32)
    Wo2 = np.asarray(inputs["L0_ffn_Wo"], dtype=np.float32)
    bo = np.asarray(inputs["L0_ffn_bo"], dtype=np.float32)

    shared = dict(
        wqk=np.ascontiguousarray(np.concatenate([Wq, Wk], axis=0).T),
        wv=np.ascontiguousarray(Wv.T),
        wo=np.ascontiguousarray(Wo.T),
        gamma=gamma.reshape(1, FD), beta=beta.reshape(1, FD),
        wp1=Wp1, bp1=bp1.reshape(1, NAV), wp2=Wp2, bp2=bp2.reshape(1, E),
        W1=W1, W2=W2,
        b1t=np.ascontiguousarray(b1.reshape(E, F // 128, 128).transpose(0, 2, 1)),
        b2t=np.ascontiguousarray(b2.reshape(E, D // 128, 128).transpose(0, 2, 1)),
        wo2=Wo2, bo=bo.reshape(1, D),
    )

    in_maps = []
    for c in range(8):
        b, hf = c // 2, c % 2
        if hf == 1:
            xp = X[b]
            cm = np.ones((1, NCH), dtype=np.float32)
        else:
            xp = np.concatenate([X[b, T // 2:], X[b, :T // 2]], axis=0)
            cm = np.ones((1, NCH), dtype=np.float32)
            cm[0, :NCH // 2] = 0.0
        m = dict(shared)
        m["xp"] = np.ascontiguousarray(xp)
        m["xr"] = np.ascontiguousarray(X[b])
        m["cmask"] = cm
        in_maps.append(m)

    if not _NC_CACHE:
        _NC_CACHE.append(build_nc())
    nc = _NC_CACHE[0]
    res = run_bass_kernel_spmd(nc, in_maps, core_ids=list(range(8)))

    outp = np.empty((B, T, D), dtype=np.float32)
    for c in range(8):
        b, hf = c // 2, c % 2
        outp[b, hf * (T // 2):(hf + 1) * (T // 2)] = res.results[c]["out"]
    return outp


# version 03
# speedup vs baseline: 1.0331x; 1.0331x over previous
import numpy as np
import concourse.bass as bass
import concourse.mybir as mybir
from concourse.bass_utils import run_bass_kernel_spmd

F32 = mybir.dt.float32
BF16 = mybir.dt.bfloat16
I32 = mybir.dt.int32
AF = mybir.ActivationFunctionType
ALU = mybir.AluOpType
AX = mybir.AxisListType

B, T, D = 4, 4096, 1024
H, FD, HD = 16, 16, 64
C = 128
NCH = T // C
F = 2048
E = 8
NAV = 128
EPS_LN = 1e-5
EPS_Z = 1e-10


class Sem:
    def __init__(self, h):
        self.h = h
        self.n = 0


def inc(ins, s, k=1):
    ins.then_inc(s.h, k)
    s.n += k
    return ins


def wge(eng, s, n=None):
    n = s.n if n is None else n
    assert n <= s.n, "wait on not-yet-emitted increment"
    if n > 0:
        eng.wait_ge(s.h, n)


class AD:
    def __init__(self, e):
        self._e = e

    def __getattr__(self, name):
        fn = getattr(self._e, name)
        if name in ("wait_ge", "drain") or not callable(fn):
            return fn
        def wrapped(*a, **k):
            self._e.drain()
            return fn(*a, **k)
        return wrapped


SM_GAMMA, SM_BETA, SM_BP1, SM_BP2 = 0, 16, 32, 160
SM_BO, SM_CM, SM_LG, SM_OH, SM_MX = 168, 1192, 1224, 1232, 1240
SM_L1, SM_G, SM_BT, SM_ONE1, SM_ONES = 1248, 1376, 1504, 2528, 2530
SMF_N = 2688
SB_OH, SB_BT = 0, 8
SMB_N = 1152

A_WQK, A_WV, A_WO = 0, 4096, 12288
A_XB, A_XT = 20480, 21504
A_XNQ, A_XNK, A_XNQT, A_XNKT = 22528, 23552, 24576, 25600
A_QF, A_KF, A_QFT = 26624, 30720, 34816
A_VSB, A_ATTM, A_OSB, A_OT, A_STB = 38912, 39952, 42000, 43024, 44048
A_W1E, A_W2E, A_HSB, A_X1T, A_RT, A_WO2 = 0, 16384, 32768, 40960, 45056, 49152
ARENA_N = 57344

AF_QK, AF_ZY, AF_ZZ, AF_X1R, AF_WP1, AF_XRT, AF_WP2 = 0, 512, 1024, 1664, 2688, 3712, 3728
AF_X2, AF_W2S = 0, 2048
ARENAF_N = 4096


def build_nc():
    nc = bass.Bass("TRN2", target_bir_lowering=False)
    d = {}
    for name, shape in [
        ("xp", [T, D]), ("xr", [T, D]), ("cmask", [1, NCH]),
        ("wqk", [D, 512]), ("wv", [D, D]), ("wo", [D, D]),
        ("gamma", [1, FD]), ("beta", [1, FD]),
        ("wp1", [D, NAV]), ("bp1", [1, NAV]), ("wp2", [NAV, E]), ("bp2", [1, E]),
        ("W1", [E, D, F]), ("W2", [E, F, D]),
        ("b1t", [E, 128, F // 128]), ("b2t", [E, 128, D // 128]),
        ("wo2", [D, D]), ("bo", [1, D]),
    ]:
        d[name] = nc.dram_tensor(name, shape, F32, kind="ExternalInput").ap()
    out_h = nc.dram_tensor("out", [T // 2, D], F32, kind="ExternalOutput").ap()

    from contextlib import ExitStack
    with ExitStack() as stack:
        en = stack.enter_context
        AR = en(nc.sbuf_tensor("arena", [128, ARENA_N], BF16))
        ARF = en(nc.sbuf_tensor("arf", [128, ARENAF_N], F32))
        xf = en(nc.sbuf_tensor("xf", [128, 2 * D], F32))
        cent = en(nc.sbuf_tensor("cent", [128, 1024], F32))
        rbuf = en(nc.sbuf_tensor("rbuf", [128, 1024], F32))
        stats = en(nc.sbuf_tensor("stats", [128, 16], F32))
        wstage = en(nc.sbuf_tensor("wstage", [128, 2 * 2048], F32))
        x1 = en(nc.sbuf_tensor("x1", [128, 16 * 1024], BF16))
        smf = en(nc.sbuf_tensor("smf", [1, SMF_N], F32))
        smb = en(nc.sbuf_tensor("smb", [1, SMB_N], BF16))
        ident = en(nc.sbuf_tensor("ident", [128, 128], BF16))
        identf = en(nc.sbuf_tensor("identf", [128, 128], F32))
        masku = en(nc.sbuf_tensor("masku", [128, 128], BF16))
        ones1 = en(nc.sbuf_tensor("ones1", [1, 128], BF16))
        ioti = en(nc.sbuf_tensor("ioti", [128, 128], I32))
        Ksa = en(nc.sbuf_tensor("Ksa", [128, 32], F32))
        Ksb = en(nc.sbuf_tensor("Ksb", [128, 32], BF16))
        onescol = en(nc.sbuf_tensor("onescol", [128, 1], BF16))
        gbr = en(nc.sbuf_tensor("gbr", [128, 32], F32))
        crep = en(nc.sbuf_tensor("crep", [128, 64], F32))
        mrep = en(nc.sbuf_tensor("mrep", [128, 8], F32))
        b1e = en(nc.sbuf_tensor("b1e", [128, 16], F32))
        b2e = en(nc.sbuf_tensor("b2e", [128, 8], F32))
        b2eb = en(nc.sbuf_tensor("b2eb", [128, 8], BF16))
        ps_state = en(nc.psum_tensor("ps_state", [128, 2048], F32))
        ps_m = en(nc.psum_tensor("ps_m", [128, 512], F32))
        ps_a = en(nc.psum_tensor("ps_a", [128, 512], F32))
        ps_b = en(nc.psum_tensor("ps_b", [128, 512], F32))
        ps_o = en(nc.psum_tensor("ps_o", [128, 512], F32))
        sems = {}
        for nm in ("sXa", "sXb", "sW0", "sW1", "sM", "sG0", "sG1",
                   "sO0", "sO1", "sT", "sS", "sV", "sI"):
            sems[nm] = Sem(en(nc.semaphore(nm)))
        blk = en(nc.Block())
        sXa, sXb, sW0, sW1, sM = (sems["sXa"], sems["sXb"], sems["sW0"],
                                  sems["sW1"], sems["sM"])
        sG0, sG1, sO0, sO1 = sems["sG0"], sems["sG1"], sems["sO0"], sems["sO1"]
        sT, sS, sV, sI = sems["sT"], sems["sS"], sems["sV"], sems["sI"]
        marks = {}
        started = set()

        def VEC(f):
            return blk.vector(lambda eng: f(AD(eng)))

        def SCA(f):
            return blk.scalar(f)

        def a(off, n):
            return AR[:, off:off + n]

        def af(off, n):
            return ARF[:, off:off + n]

        @blk.gpsimd
        def _(eng):
            inc(eng.iota(ioti[:], pattern=[[1, 128]], base=0, channel_multiplier=-1), sI)

        @VEC
        def _(eng):
            wge(eng, sI)
            eng.tensor_scalar(ident[:], ioti[:], 0, None, ALU.is_equal)
            eng.tensor_scalar(identf[:], ioti[:], 0, None, ALU.is_equal)
            eng.tensor_scalar(masku[:], ioti[:], 0, None, ALU.is_ge)
            eng.memset(ones1[:], 1.0)
            eng.memset(smf[0:1, SM_ONE1:SM_ONE1 + 1], 1.0)
            eng.memset(smf[0:1, SM_ONES:SM_ONES + 128], 1.0)
            eng.memset(a(A_XNQ, 1024), 0.0)
            eng.memset(a(A_XNK, 1024), 0.0)
            eng.memset(Ksa[:], 0.0)
            eng.memset(onescol[:], 1.0)
            inc(eng.memset(a(A_VSB, 1040), 1.0), sV)

        @blk.sync
        def _(eng):
            inc(eng.dma_start(smf[0:1, SM_GAMMA:SM_GAMMA + 16], d["gamma"]), sM, 16)
            inc(eng.dma_start(smf[0:1, SM_BETA:SM_BETA + 16], d["beta"]), sM, 16)
            inc(eng.dma_start(smf[0:1, SM_BP1:SM_BP1 + 128], d["bp1"]), sM, 16)
            inc(eng.dma_start(smf[0:1, SM_BP2:SM_BP2 + 8], d["bp2"]), sM, 16)
            inc(eng.dma_start(smf[0:1, SM_BO:SM_BO + 1024], d["bo"]), sM, 16)
            inc(eng.dma_start(smf[0:1, SM_CM:SM_CM + 32], d["cmask"]), sM, 16)
            for i in range(8):
                inc(eng.dma_start(af(AF_WP1 + i * 128, 128),
                                  d["wp1"].rearrange("(a p) n -> a p n", p=128)[i]), sM, 16)
            inc(eng.dma_start(af(AF_WP2, 8), d["wp2"].rearrange("p n -> p n")), sM, 16)
        mark_small = sM.n

        @blk.tensor
        def _(eng):
            wge(eng, sM, mark_small)
            wge(eng, sV)
            onesf = smf[0:1, SM_ONES:SM_ONES + 128]
            eng.matmul(ps_a[:, 0:16], onesf, smf[0:1, SM_GAMMA:SM_GAMMA + 16], start=True, stop=True)
            eng.matmul(ps_a[:, 16:32], onesf, smf[0:1, SM_BETA:SM_BETA + 16], start=True, stop=True)
            inc(eng.matmul(ps_a[:, 32:64], onesf, smf[0:1, SM_CM:SM_CM + 32], start=True, stop=True), sT)

        @SCA
        def _(eng):
            wge(eng, sT)
            eng.copy(gbr[:, 0:32], ps_a[:, 0:32])
            eng.copy(crep[:, 0:32], ps_a[:, 32:64])
            inc(eng.memzero(crep[:, 32:64]), sS)

        @VEC
        def _(eng):
            wge(eng, sS)
            inc(eng.tensor_scalar(crep[:, 32:64], crep[:, 32:64], 1.0, None, ALU.add), sV)
        mark_rep = sV.n

        wload = ([("wqk", A_WQK, 512, i) for i in range(8)] +
                 [("wv", A_WV, 1024, i) for i in range(8)] +
                 [("wo", A_WO, 1024, i) for i in range(8)] +
                 [("wo2", A_WO2, 1024, i) for i in range(8)])
        for j, (nm, base, ncols, i) in enumerate(wload):
            bufo = (j % 2) * 2048
            sw = sW0 if j % 2 == 0 else sW1
            @blk.sync
            def _(eng, nm=nm, i=i, bufo=bufo, ncols=ncols, j=j):
                mk = marks.get(("wst", j % 2))
                if mk is not None:
                    wge(eng, sS, mk)
                inc(eng.dma_start(wstage[:, bufo:bufo + ncols],
                                  d[nm].rearrange("(a p) n -> a p n", p=128)[i]), sw, 16)
            @SCA
            def _(eng, base=base, i=i, bufo=bufo, ncols=ncols, sw=sw):
                wge(eng, sw)
                inc(eng.copy(a(base + i * ncols, ncols), wstage[:, bufo:bufo + ncols]), sS)
            marks[("wst", j % 2)] = sS.n
        mark_p0cast = sS.n

        def kchunk(xin, ch, do_q, do_copies, mask_off, x1_dest, phase):
            bi = ch % 2
            sL = sXa if bi == 0 else sXb
            xfv = xf[:, bi * D:(bi + 1) * D]
            xb = a(A_XB, 1024)
            xt = a(A_XT, 1024)

            @blk.sync
            def _(eng):
                mk = marks.get(("xf", bi))
                if mk is not None:
                    wge(eng, mk[0], mk[1])
                inc(eng.dma_start(xfv, xin[ch * 128:(ch + 1) * 128, :]), sL, 16)

            @SCA
            def _(eng):
                wge(eng, sL)
                wge(eng, sT)
                inc(eng.copy(xb, xfv), sS)

            @blk.tensor
            def _(eng):
                wge(eng, sS)
                wge(eng, sV)
                for k in range(8):
                    ps = ps_a if k < 4 else ps_b
                    ins = eng.matmul(ps[:, (k % 4) * 128:(k % 4 + 1) * 128],
                                     xb[:, k * 128:(k + 1) * 128], ident[:],
                                     start=True, stop=True)
                inc(ins, sT)

            @SCA
            def _(eng):
                wge(eng, sT)
                eng.copy(xt[:, 0:512], ps_a[:])
                inc(eng.copy(xt[:, 512:1024], ps_b[:]), sS)

            @blk.tensor
            def _(eng):
                wge(eng, sS)
                for dsl in range(8):
                    eng.matmul(ps_a[:], xt[:, dsl * 128:(dsl + 1) * 128],
                               a(A_WQK + dsl * 512, 512), start=(dsl == 0), stop=(dsl == 7))
                for dsl in range(8):
                    eng.matmul(ps_b[:], xt[:, dsl * 128:(dsl + 1) * 128],
                               a(A_WV + dsl * 1024, 512), start=(dsl == 0), stop=(dsl == 7))
                for dsl in range(8):
                    ins = eng.matmul(ps_o[:], xt[:, dsl * 128:(dsl + 1) * 128],
                                     a(A_WV + dsl * 1024 + 512, 512), start=(dsl == 0), stop=(dsl == 7))
                inc(ins, sT)

            @SCA
            def _(eng):
                wge(eng, sT)
                eng.copy(af(AF_QK, 512), ps_a[:])
                v3 = a(A_VSB, 1040).rearrange("p (h c) -> p h c", c=65)
                eng.copy(v3[:, 0:8, 0:64], ps_b[:].rearrange("p (h c) -> p h c", c=64))
                inc(eng.copy(v3[:, 8:16, 0:64], ps_o[:].rearrange("p (h c) -> p h c", c=64)), sS)
            if not do_q:
                marks[("xf", bi)] = (sS, sS.n)

            zzv = af(AF_ZZ, 640)
            @VEC
            def _(eng):
                wge(eng, sS)
                q3 = af(AF_QK, 512).rearrange("p (g f) -> p g f", f=16)
                mu = zzv[:, 512:544].rearrange("p (g o) -> p g o", o=1)
                eng.reduce_sum(mu, q3, axis=AX.X)
                eng.tensor_scalar(zzv[:, 512:544], zzv[:, 512:544], 1.0 / 16, None, ALU.mult)
                c3 = cent[:, 0:512].rearrange("p (g f) -> p g f", f=16)
                eng.tensor_tensor(c3, q3, mu.broadcast_to([128, 32, 16]), ALU.subtract)
                s3 = cent[:, 512:1024].rearrange("p (g f) -> p g f", f=16)
                eng.tensor_tensor(s3, c3, c3, ALU.mult)
                vs = zzv[:, 544:576].rearrange("p (g o) -> p g o", o=1)
                eng.reduce_sum(vs, s3, axis=AX.X)
                inc(eng.tensor_scalar(zzv[:, 544:576], zzv[:, 544:576],
                                      1.0 / 16, EPS_LN, ALU.mult, ALU.add), sV)

            @SCA
            def _(eng):
                wge(eng, sV)
                inc(eng.activation(zzv[:, 608:640], zzv[:, 544:576], AF.Sqrt), sS)

            @VEC
            def _(eng):
                wge(eng, sS)
                eng.reciprocal(zzv[:, 576:608], zzv[:, 608:640])
                c3 = cent[:, 0:512].rearrange("p (g f) -> p g f", f=16)
                rs = zzv[:, 576:608].rearrange("p (g o) -> p g o", o=1)
                eng.tensor_tensor(c3, c3, rs.broadcast_to([128, 32, 16]), ALU.mult)
                gam = gbr[:, 0:16].unsqueeze(1).broadcast_to([128, 16, 16])
                bet = gbr[:, 16:32].unsqueeze(1).broadcast_to([128, 16, 16])
                xk3 = a(A_XNK, 1024).rearrange("p (h f) -> p h f", f=64)[:, :, 0:16]
                ck = cent[:, 256:512].rearrange("p (h f) -> p h f", f=16)
                eng.tensor_tensor(xk3, ck, gam, ALU.mult)
                eng.tensor_tensor(xk3, xk3, bet, ALU.add)
                mk = crep[:, mask_off + ch:mask_off + ch + 1]
                eng.tensor_scalar(xk3, xk3, mk, None, ALU.mult)
                kk = a(A_XNK, 1024).rearrange("p (h f) -> p h f", f=64)[:, :, 0:16]
                eng.tensor_tensor(a(A_KF, 4096).rearrange("p (h i j) -> p h i j", i=16, j=16),
                                  kk.unsqueeze(3).broadcast_to([128, 16, 16, 16]),
                                  kk.unsqueeze(2).broadcast_to([128, 16, 16, 16]), ALU.mult)
                if do_q:
                    xq3 = a(A_XNQ, 1024).rearrange("p (h f) -> p h f", f=64)[:, :, 0:16]
                    cq = cent[:, 0:256].rearrange("p (h f) -> p h f", f=16)
                    eng.tensor_tensor(xq3, cq, gam, ALU.mult)
                    eng.tensor_tensor(xq3, xq3, bet, ALU.add)
                    qq = a(A_XNQ, 1024).rearrange("p (h f) -> p h f", f=64)[:, :, 0:16]
                    eng.tensor_tensor(a(A_QF, 4096).rearrange("p (h i j) -> p h i j", i=16, j=16),
                                      qq.unsqueeze(3).broadcast_to([128, 16, 16, 16]),
                                      qq.unsqueeze(2).broadcast_to([128, 16, 16, 16]), ALU.mult)
                inc(eng.engine_nop(), sV)

            if do_copies:
                @SCA
                def _(eng):
                    wge(eng, sT)
                    wge(eng, sV)
                    eng.activation(a(A_STB, 2048), ps_state[:], AF.Copy, bias=0.0, scale=1.0 / 16)
                    inc(eng.activation(Ksb[:], Ksa[:], AF.Copy, bias=0.0, scale=1.0 / 16), sS)
            mark_copies = sS.n

            if do_q:
                for src, dst in ((A_XNQ, A_XNQT), (A_XNK, A_XNKT)):
                    @blk.tensor
                    def _(eng, src=src):
                        wge(eng, sS)
                        wge(eng, sV)
                        for k in range(8):
                            ps = ps_a if k < 4 else ps_b
                            ins = eng.matmul(ps[:, (k % 4) * 128:(k % 4 + 1) * 128],
                                             a(src + k * 128, 128), ident[:], start=True, stop=True)
                        inc(ins, sT)
                    @SCA
                    def _(eng, dst=dst):
                        wge(eng, sT)
                        eng.copy(a(dst, 512), ps_a[:])
                        inc(eng.copy(a(dst + 512, 512), ps_b[:]), sS)
                for wv_ in range(4):
                    @blk.tensor
                    def _(eng, wv_=wv_):
                        wge(eng, sS)
                        for k in range(8):
                            ps = ps_a if k < 4 else ps_b
                            ins = eng.matmul(ps[:, (k % 4) * 128:(k % 4 + 1) * 128],
                                             a(A_QF + wv_ * 1024 + k * 128, 128), ident[:],
                                             start=True, stop=True)
                        inc(ins, sT)
                    @SCA
                    def _(eng, wv_=wv_):
                        wge(eng, sT)
                        eng.copy(a(A_QFT + wv_ * 1024, 512), ps_a[:])
                        inc(eng.copy(a(A_QFT + wv_ * 1024 + 512, 512), ps_b[:]), sS)

            @blk.tensor
            def _(eng):
                if do_copies:
                    wge(eng, sS, mark_copies)
                wge(eng, sV)
                for h in range(16):
                    for kc in range(2):
                        col = (h * 2 + kc) * 64
                        key = (phase, "st", col // 512)
                        st = key not in started
                        started.add(key)
                        eng.matmul(ps_state[:, col:col + 64],
                                   a(A_KF + h * 256 + kc * 128, 128),
                                   a(A_VSB + h * 65, 64), start=st, stop=False,
                                   skip_group_check=True)
                ins = None
                for h in range(16):
                    for kc in range(2):
                        col = h * 2 + kc
                        ins = eng.matmul(ps_m[:, col:col + 1],
                                         a(A_KF + h * 256 + kc * 128, 128),
                                         onescol[:], start=True, stop=True)
                inc(ins, sT)

            @VEC
            def _(eng):
                wge(eng, sT)
                inc(eng.tensor_tensor(Ksa[:], Ksa[:], ps_m[:, 0:32], ALU.add), sV)

            if not do_q:
                return

            xnqT = a(A_XNQT, 1024)
            xnkT = a(A_XNKT, 1024)
            for key in (("attev", 0), ("attev", 1), ("oev", 0), ("oev", 1)):
                marks.pop(key, None)
            for h in range(16):
                bank = ps_a if h % 2 == 0 else ps_b
                @blk.tensor
                def _(eng, h=h, bank=bank):
                    mk = marks.get(("attev", h % 2))
                    if mk is not None:
                        wge(eng, sS, mk)
                    else:
                        wge(eng, sS)
                    off = 64 * (h % 2)
                    tl = (h // 2) * 128
                    inc(eng.matmul(bank[:, 0:128], xnkT[off:off + 64, tl:tl + 128],
                                   xnqT[off:off + 64, tl:tl + 128], start=True, stop=True), sT)
                @SCA
                def _(eng, h=h, bank=bank):
                    wge(eng, sT)
                    inc(eng.activation(a(A_ATTM + h * 128, 128), bank[:, 0:128],
                                       AF.Square, bias=0.0, scale=0.25), sS)
                marks[("attev", h % 2)] = sS.n

            @VEC
            def _(eng):
                wge(eng, sS)
                am = a(A_ATTM, 2048).rearrange("p (h t) -> p h t", t=128)
                inc(eng.tensor_tensor(am, am, masku[:].unsqueeze(1).broadcast_to([128, 16, 128]),
                                      ALU.mult), sV)

            for h in range(16):
                bank = ps_a if h % 2 == 0 else ps_b
                @blk.tensor
                def _(eng, h=h, bank=bank):
                    mk = marks.get(("oev", h % 2))
                    if mk is not None:
                        wge(eng, sS, mk)
                    wge(eng, sV)
                    eng.matmul(bank[:, 0:65], a(A_ATTM + h * 128, 128),
                               a(A_VSB + h * 65, 65), start=True, stop=False)
                    for kc in range(2):
                        eng.matmul(bank[:, 0:64], a(A_QFT + h * 256 + kc * 128, 128),
                                   a(A_STB + (h * 2 + kc) * 64, 64),
                                   start=False, stop=False)
                    for kc in range(2):
                        ins = eng.matmul(bank[:, 64:65], a(A_QFT + h * 256 + kc * 128, 128),
                                         Ksb[:, h * 2 + kc:h * 2 + kc + 1],
                                         start=False, stop=(kc == 1))
                    inc(ins, sT)
                @SCA
                def _(eng, h=h, bank=bank):
                    wge(eng, sT)
                    eng.copy(a(A_OSB + h * 64, 64), bank[:, 0:64])
                    inc(eng.copy(af(AF_ZY + h, 1), bank[:, 64:65]), sS)
                marks[("oev", h % 2)] = sS.n

            @VEC
            def _(eng):
                wge(eng, sS)
                eng.tensor_scalar(zzv[:, 16:32], af(AF_ZY, 16), EPS_Z, None, ALU.add)
                eng.reciprocal(zzv[:, 32:48], zzv[:, 16:32])
                o3 = a(A_OSB, 1024).rearrange("p (h c) -> p h c", c=64)
                zr = zzv[:, 32:48].rearrange("p (h o) -> p h o", o=1)
                inc(eng.tensor_tensor(o3, o3, zr.broadcast_to([128, 16, 64]), ALU.mult), sV)

            @blk.tensor
            def _(eng):
                wge(eng, sV)
                wge(eng, sS)
                for k in range(8):
                    ps = ps_a if k < 4 else ps_b
                    ins = eng.matmul(ps[:, (k % 4) * 128:(k % 4 + 1) * 128],
                                     a(A_OSB + k * 128, 128), ident[:], start=True, stop=True)
                inc(ins, sT)
            @SCA
            def _(eng):
                wge(eng, sT)
                eng.copy(a(A_OT, 512), ps_a[:])
                inc(eng.copy(a(A_OT + 512, 512), ps_b[:]), sS)
            @blk.tensor
            def _(eng):
                wge(eng, sS)
                for bk in range(8):
                    for w in range(2):
                        ps = ps_a if w == 0 else ps_b
                        ins = eng.matmul(ps[:], a(A_OT + bk * 128, 128),
                                         a(A_WO + bk * 1024 + w * 512, 512),
                                         start=(bk == 0), stop=(bk == 7))
                inc(ins, sT)

            @VEC
            def _(eng):
                wge(eng, sT)
                eng.tensor_tensor(rbuf[:, 0:512], ps_a[:], xfv[:, 0:512], ALU.add)
                eng.tensor_tensor(rbuf[:, 512:1024], ps_b[:], xfv[:, 512:1024], ALU.add)
                eng.reduce_sum(stats[:, 0:1], rbuf[:], axis=AX.X)
                eng.tensor_tensor(cent[:], rbuf[:], rbuf[:], ALU.mult)
                eng.reduce_sum(stats[:, 1:2], cent[:], axis=AX.X)
                eng.tensor_scalar(stats[:, 2:3], stats[:, 0:1], 1.0 / D, None, ALU.mult)
                eng.tensor_tensor(stats[:, 3:4], stats[:, 2:3], stats[:, 2:3], ALU.mult)
                eng.scalar_tensor_tensor(stats[:, 4:5], stats[:, 1:2], 1.0 / D, stats[:, 3:4],
                                         ALU.mult, ALU.subtract)
                inc(eng.tensor_scalar(stats[:, 4:5], stats[:, 4:5], EPS_LN, None, ALU.add), sV)
            @SCA
            def _(eng):
                wge(eng, sV)
                inc(eng.activation(stats[:, 7:8], stats[:, 4:5], AF.Sqrt), sS)
            @VEC
            def _(eng):
                wge(eng, sS)
                eng.reciprocal(stats[:, 5:6], stats[:, 7:8])
                eng.tensor_tensor(stats[:, 6:7], stats[:, 2:3], stats[:, 5:6], ALU.mult)
                eng.tensor_scalar(stats[:, 6:7], stats[:, 6:7], -1.0, None, ALU.mult)
                dst, dco = x1_dest
                inc(eng.scalar_tensor_tensor(dst[:, dco:dco + 1024], rbuf[:], stats[:, 5:6],
                                             stats[:, 6:7].broadcast_to([128, 1024]),
                                             ALU.mult, ALU.add), sV)
            marks[("xf", bi)] = (sV, sV.n)

        x1r = af(AF_X1R, 1024)
        for ch in range(NCH):
            kchunk(d["xr"], ch, do_q=(ch == NCH - 1), do_copies=(ch == NCH - 1),
                   mask_off=32, x1_dest=(ARF, AF_X1R), phase=1)

        xrt = af(AF_XRT, 16)
        @blk.tensor
        def _(eng):
            wge(eng, sV)
            wge(eng, sS)
            for k in range(8):
                ps = ps_a if k < 4 else ps_b
                ins = eng.matmul(ps[:, (k % 4) * 128:(k % 4 + 1) * 128],
                                 x1r[:, k * 128:(k + 1) * 128], identf[:], start=True, stop=True)
            inc(ins, sT)
        @SCA
        def _(eng):
            wge(eng, sT)
            ins = None
            for k in range(8):
                ps = ps_a if k < 4 else ps_b
                ins = eng.copy(xrt[:, k:k + 1], ps[:, (k % 4) * 128 + 127:(k % 4) * 128 + 128])
            inc(ins, sS)
        @blk.tensor
        def _(eng):
            wge(eng, sS)
            for k in range(8):
                ins = eng.matmul(ps_o[0:1, 0:128], xrt[:, k:k + 1],
                                 af(AF_WP1 + k * 128, 128), start=(k == 0), stop=(k == 7))
            inc(ins, sT)
        @VEC
        def _(eng):
            wge(eng, sT)
            inc(eng.tensor_tensor(smf[0:1, SM_L1:SM_L1 + 128], ps_o[0:1, 0:128],
                                  smf[0:1, SM_BP1:SM_BP1 + 128], ALU.add), sV)
        @SCA
        def _(eng):
            wge(eng, sV)
            inc(eng.activation(smf[0:1, SM_G:SM_G + 128], smf[0:1, SM_L1:SM_L1 + 128], AF.Gelu), sS)
        @blk.tensor
        def _(eng):
            wge(eng, sS)
            inc(eng.matmul(ps_o[:, 128:129], smf[0:1, SM_G:SM_G + 128],
                           smf[0:1, SM_ONE1:SM_ONE1 + 1], start=True, stop=True), sT)
        @SCA
        def _(eng):
            wge(eng, sT)
            inc(eng.copy(xrt[:, 8:9], ps_o[:, 128:129]), sS)
        @blk.tensor
        def _(eng):
            wge(eng, sS)
            inc(eng.matmul(ps_o[0:1, 136:144], xrt[:, 8:9], af(AF_WP2, 8), start=True, stop=True), sT)
        @VEC
        def _(eng):
            wge(eng, sT)
            eng.tensor_tensor(smf[0:1, SM_LG:SM_LG + 8], ps_o[0:1, 136:144],
                              smf[0:1, SM_BP2:SM_BP2 + 8], ALU.add)
            eng.reduce_max(smf[0:1, SM_MX:SM_MX + 1], smf[0:1, SM_LG:SM_LG + 8], axis=AX.X)
            inc(eng.tensor_tensor(smf[0:1, SM_OH:SM_OH + 8], smf[0:1, SM_LG:SM_LG + 8],
                                  smf[0:1, SM_MX:SM_MX + 1].broadcast_to([1, 8]), ALU.is_equal), sV)
        @SCA
        def _(eng):
            wge(eng, sV)
            inc(eng.copy(smb[0:1, SB_OH:SB_OH + 8], smf[0:1, SM_OH:SM_OH + 8]), sS)
        @blk.tensor
        def _(eng):
            wge(eng, sS)
            inc(eng.matmul(ps_o[:, 144:152], ones1[:], smb[0:1, SB_OH:SB_OH + 8],
                           start=True, stop=True), sT)
        @SCA
        def _(eng):
            wge(eng, sT)
            inc(eng.copy(mrep[:], ps_o[:, 144:152]), sS)
        mark_mrep = sS.n

        @blk.sync
        def _(eng):
            wge(eng, sT)
            wge(eng, sV)
            inc(eng.dma_start(af(AF_W2S, 128).rearrange("p (e f) -> p e f", e=8),
                              d["b1t"].rearrange("e p f -> p e f")), sM, 16)
            inc(eng.dma_start(af(AF_W2S + 128, 64).rearrange("p (e f) -> p e f", e=8),
                              d["b2t"].rearrange("e p f -> p e f")), sM, 16)
        @VEC
        def _(eng):
            wge(eng, sM)
            wge(eng, sS, mark_mrep)
            b13 = af(AF_W2S, 128).rearrange("p (e f) -> p e f", e=8)
            b23 = af(AF_W2S + 128, 64).rearrange("p (e f) -> p e f", e=8)
            for e in range(E):
                if e == 0:
                    eng.tensor_scalar(b1e[:], b13[:, 0, :], mrep[:, 0:1], None, ALU.mult)
                    eng.tensor_scalar(b2e[:], b23[:, 0, :], mrep[:, 0:1], None, ALU.mult)
                else:
                    eng.scalar_tensor_tensor(b1e[:], b13[:, e, :], mrep[:, e:e + 1], b1e[:],
                                             ALU.mult, ALU.add)
                    eng.scalar_tensor_tensor(b2e[:], b23[:, e, :], mrep[:, e:e + 1], b2e[:],
                                             ALU.mult, ALU.add)
            inc(eng.engine_nop(), sV)
        @SCA
        def _(eng):
            wge(eng, sV)
            inc(eng.copy(b2eb[:], b2e[:]), sS)
        @blk.tensor
        def _(eng):
            wge(eng, sS)
            for w in range(2):
                for dsl in range(8):
                    ins = eng.matmul((ps_a if w == 0 else ps_b)[0:1, 0:512],
                                     b2eb[:, dsl:dsl + 1],
                                     a(A_WO2 + dsl * 1024 + w * 512, 512),
                                     start=(dsl == 0), stop=(dsl == 7))
            inc(ins, sT)
        mark_router_t = sT.n
        @VEC
        def _(eng):
            wge(eng, sT)
            eng.tensor_tensor(smf[0:1, SM_BT:SM_BT + 512], ps_a[0:1, 0:512],
                              smf[0:1, SM_BO:SM_BO + 512], ALU.add)
            inc(eng.tensor_tensor(smf[0:1, SM_BT + 512:SM_BT + 1024], ps_b[0:1, 0:512],
                                  smf[0:1, SM_BO + 512:SM_BO + 1024], ALU.add), sV)
        @SCA
        def _(eng):
            wge(eng, sV)
            inc(eng.copy(smb[0:1, SB_BT:SB_BT + 1024], smf[0:1, SM_BT:SM_BT + 1024]), sS)

        @VEC
        def _(eng):
            inc(eng.memset(Ksa[:], 0.0), sV)

        for ch in range(NCH):
            kchunk(d["xp"], ch, do_q=(ch >= 16), do_copies=(ch >= 16),
                   mask_off=0, x1_dest=(x1, (ch - 16) * 1024), phase=2)

        mark_w1 = {}
        for dsl in range(8):
            for e in range(E):
                j = dsl * 8 + e
                bufo = (j % 2) * 2048
                sw = sW0 if j % 2 == 0 else sW1
                @blk.sync
                def _(eng, dsl=dsl, e=e, bufo=bufo, j=j, sw=sw):
                    mk = marks.get(("w1s", j % 2))
                    if mk is not None:
                        wge(eng, sV, mk)
                    else:
                        wge(eng, sS, mark_p0cast)
                    inc(eng.dma_start(wstage[:, bufo:bufo + 2048],
                                      d["W1"][e].rearrange("(a p) f -> a p f", p=128)[dsl]), sw, 16)
                @VEC
                def _(eng, dsl=dsl, e=e, bufo=bufo, sw=sw):
                    wge(eng, sw)
                    if e == 0:
                        wge(eng, sS, mark_mrep)
                        ins = eng.tensor_scalar(a(A_W1E + dsl * 2048, 2048),
                                                wstage[:, bufo:bufo + 2048],
                                                mrep[:, 0:1], None, ALU.mult)
                    else:
                        ins = eng.scalar_tensor_tensor(a(A_W1E + dsl * 2048, 2048),
                                                       wstage[:, bufo:bufo + 2048],
                                                       mrep[:, e:e + 1], a(A_W1E + dsl * 2048, 2048),
                                                       ALU.mult, ALU.add)
                    inc(ins, sV)
                marks[("w1s", j % 2)] = sV.n
        mark_w1done = sV.n

        for fsl in range(16):
            for e in range(E):
                j = fsl * 8 + e
                bufo = (j % 2) * 1024
                sg = sG0 if j % 2 == 0 else sG1
                @blk.gpsimd
                def _(eng, fsl=fsl, e=e, bufo=bufo, j=j, sg=sg):
                    mk = marks.get(("w2s", j % 2))
                    if mk is not None:
                        wge(eng, sV, mk)
                    else:
                        wge(eng, sT, mark_router_t)
                    inc(eng.dma_start(af(AF_W2S + bufo, 1024),
                                      d["W2"][e].rearrange("(a p) f -> a p f", p=128)[fsl]), sg, 16)
                @VEC
                def _(eng, fsl=fsl, e=e, bufo=bufo, sg=sg):
                    wge(eng, sg)
                    if e == 0:
                        ins = eng.tensor_scalar(a(A_W2E + fsl * 1024, 1024),
                                                af(AF_W2S + bufo, 1024),
                                                mrep[:, 0:1], None, ALU.mult)
                    else:
                        ins = eng.scalar_tensor_tensor(a(A_W2E + fsl * 1024, 1024),
                                                       af(AF_W2S + bufo, 1024),
                                                       mrep[:, e:e + 1], a(A_W2E + fsl * 1024, 1024),
                                                       ALU.mult, ALU.add)
                    inc(ins, sV)
                marks[("w2s", j % 2)] = sV.n
        mark_w2done = sV.n

        for stile in range(4):
            for key in (("hev", 0), ("hev", 1), ("rev", 0), ("rev", 1)):
                marks.pop(key, None)
            for tc in range(4):
                slot = stile * 4 + tc
                @blk.tensor
                def _(eng, slot=slot, tc=tc, stile=stile):
                    wge(eng, sS)
                    if stile == 0 and tc == 0:
                        wge(eng, sV, mark_w1done)
                    for k in range(8):
                        ps = ps_a if k < 4 else ps_b
                        ins = eng.matmul(ps[:, (k % 4) * 128:(k % 4 + 1) * 128],
                                         x1[:, slot * 1024 + k * 128: slot * 1024 + (k + 1) * 128],
                                         ident[:], start=True, stop=True)
                    inc(ins, sT)
                @SCA
                def _(eng, tc=tc):
                    wge(eng, sT)
                    x1t3 = a(A_X1T, 4096).rearrange("p (s t) -> p s t", t=512)
                    eng.copy(x1t3[:, 0:4, tc * 128:(tc + 1) * 128],
                             ps_a[:].rearrange("p (s t) -> p s t", t=128))
                    inc(eng.copy(x1t3[:, 4:8, tc * 128:(tc + 1) * 128],
                                 ps_b[:].rearrange("p (s t) -> p s t", t=128)), sS)

            for ft in range(16):
                bank = ps_a if ft % 2 == 0 else ps_b
                @blk.tensor
                def _(eng, ft=ft, bank=bank):
                    mk = marks.get(("hev", ft % 2))
                    if mk is not None:
                        wge(eng, sS, mk)
                    else:
                        wge(eng, sS)
                    for dsl in range(8):
                        ins = eng.matmul(bank[:], a(A_W1E + dsl * 2048 + ft * 128, 128),
                                         a(A_X1T + dsl * 512, 512), start=(dsl == 0), stop=(dsl == 7))
                    inc(ins, sT)
                @SCA
                def _(eng, ft=ft, bank=bank):
                    wge(eng, sT)
                    inc(eng.activation(a(A_HSB + ft * 512, 512), bank[:], AF.Gelu,
                                       bias=b1e[:, ft:ft + 1], scale=1.0), sS)
                marks[("hev", ft % 2)] = sS.n

            for dt in range(8):
                bank = ps_o if dt % 2 == 0 else ps_m
                @blk.tensor
                def _(eng, dt=dt, bank=bank):
                    mk = marks.get(("rev", dt % 2))
                    if mk is not None:
                        wge(eng, sS, mk)
                    else:
                        wge(eng, sS)
                    if dt == 0:
                        wge(eng, sV, mark_w2done)
                    for fsl in range(16):
                        ins = eng.matmul(bank[:], a(A_W2E + fsl * 1024 + dt * 128, 128),
                                         a(A_HSB + fsl * 512, 512), start=(fsl == 0), stop=(fsl == 15))
                    inc(ins, sT)
                @SCA
                def _(eng, dt=dt, bank=bank):
                    wge(eng, sT)
                    inc(eng.copy(a(A_RT + dt * 512, 512), bank[:]), sS)
                marks[("rev", dt % 2)] = sS.n

            for tc in range(4):
                slot = stile * 4 + tc
                pso0 = ps_state[:, (tc % 2) * 1024:(tc % 2) * 1024 + 512]
                pso1 = ps_state[:, (tc % 2) * 1024 + 512:(tc % 2) * 1024 + 1024]
                @blk.tensor
                def _(eng, tc=tc, pso0=pso0, pso1=pso1):
                    mk = marks.get(("o2", tc % 2))
                    if mk is not None:
                        wge(eng, sV, mk)
                    wge(eng, sS)
                    for w, pw in ((0, pso0), (1, pso1)):
                        for dsl in range(8):
                            eng.matmul(pw, a(A_RT + dsl * 512 + tc * 128, 128),
                                       a(A_WO2 + dsl * 1024 + w * 512, 512),
                                       start=(dsl == 0), stop=False)
                        ins = eng.matmul(pw, ones1[:], smb[0:1, SB_BT + w * 512:SB_BT + (w + 1) * 512],
                                         start=False, stop=True)
                    inc(ins, sT)
                @VEC
                def _(eng, slot=slot, pso0=pso0, pso1=pso1):
                    wge(eng, sT)
                    eng.scalar_tensor_tensor(rbuf[:, 0:512], pso0, 0.15,
                                             x1[:, slot * 1024:slot * 1024 + 512], ALU.mult, ALU.add)
                    eng.scalar_tensor_tensor(rbuf[:, 512:1024], pso1, 0.15,
                                             x1[:, slot * 1024 + 512:slot * 1024 + 1024],
                                             ALU.mult, ALU.add)
                    eng.reduce_sum(stats[:, 0:1], rbuf[:], axis=AX.X)
                    eng.tensor_tensor(cent[:], rbuf[:], rbuf[:], ALU.mult)
                    eng.reduce_sum(stats[:, 1:2], cent[:], axis=AX.X)
                    eng.tensor_scalar(stats[:, 2:3], stats[:, 0:1], 1.0 / D, None, ALU.mult)
                    eng.tensor_tensor(stats[:, 3:4], stats[:, 2:3], stats[:, 2:3], ALU.mult)
                    eng.scalar_tensor_tensor(stats[:, 4:5], stats[:, 1:2], 1.0 / D, stats[:, 3:4],
                                             ALU.mult, ALU.subtract)
                    inc(eng.tensor_scalar(stats[:, 4:5], stats[:, 4:5], EPS_LN / 4, None, ALU.add), sV)
                @SCA
                def _(eng):
                    wge(eng, sV)
                    inc(eng.activation(stats[:, 7:8], stats[:, 4:5], AF.Sqrt), sS)
                so = sO0 if slot % 2 == 0 else sO1
                @VEC
                def _(eng, slot=slot, so=so):
                    wge(eng, sS)
                    mko = marks.get(("x2", slot % 2))
                    if mko is not None:
                        wge(eng, so, mko)
                    eng.reciprocal(stats[:, 5:6], stats[:, 7:8])
                    eng.tensor_tensor(stats[:, 6:7], stats[:, 2:3], stats[:, 5:6], ALU.mult)
                    eng.tensor_scalar(stats[:, 6:7], stats[:, 6:7], -1.0, None, ALU.mult)
                    inc(eng.scalar_tensor_tensor(af(AF_X2 + (slot % 2) * 1024, 1024), rbuf[:],
                                                 stats[:, 5:6],
                                                 stats[:, 6:7].broadcast_to([128, 1024]),
                                                 ALU.mult, ALU.add), sV)
                marks[("o2", tc % 2)] = sV.n
                @blk.gpsimd
                def _(eng, slot=slot, so=so):
                    wge(eng, sV)
                    inc(eng.dma_start(out_h[slot * 128:(slot + 1) * 128, :],
                                      af(AF_X2 + (slot % 2) * 1024, 1024)), so, 16)
                marks[("x2", slot % 2)] = so.n

        @blk.gpsimd
        def _(eng):
            wge(eng, sO0, 8 * 16)
            wge(eng, sO1, 8 * 16)

    return nc


_NC_CACHE = []


def kernel(**inputs):
    X = np.asarray(inputs["L0_attn_hidden_states"], dtype=np.float32)
    Wq = np.asarray(inputs["L0_attn_Wq"], dtype=np.float32)
    Wk = np.asarray(inputs["L0_attn_Wk"], dtype=np.float32)
    Wv = np.asarray(inputs["L0_attn_Wv"], dtype=np.float32)
    Wo = np.asarray(inputs["L0_attn_Wo"], dtype=np.float32)
    gamma = np.asarray(inputs["L0_attn_gamma"], dtype=np.float32)
    beta = np.asarray(inputs["L0_attn_beta"], dtype=np.float32)
    Wp1 = np.asarray(inputs["L0_ffn_Wp1"], dtype=np.float32)
    bp1 = np.asarray(inputs["L0_ffn_bp1"], dtype=np.float32)
    Wp2 = np.asarray(inputs["L0_ffn_Wp2"], dtype=np.float32)
    bp2 = np.asarray(inputs["L0_ffn_bp2"], dtype=np.float32)
    W1 = np.asarray(inputs["L0_ffn_W1"], dtype=np.float32)
    b1 = np.asarray(inputs["L0_ffn_b1"], dtype=np.float32)
    W2 = np.asarray(inputs["L0_ffn_W2"], dtype=np.float32)
    b2 = np.asarray(inputs["L0_ffn_b2"], dtype=np.float32)
    Wo2 = np.asarray(inputs["L0_ffn_Wo"], dtype=np.float32)
    bo = np.asarray(inputs["L0_ffn_bo"], dtype=np.float32)

    shared = dict(
        wqk=np.ascontiguousarray(np.concatenate([Wq, Wk], axis=0).T),
        wv=np.ascontiguousarray(Wv.T),
        wo=np.ascontiguousarray(Wo.T),
        gamma=gamma.reshape(1, FD), beta=beta.reshape(1, FD),
        wp1=Wp1, bp1=bp1.reshape(1, NAV), wp2=Wp2, bp2=bp2.reshape(1, E),
        W1=W1, W2=W2,
        b1t=np.ascontiguousarray(b1.reshape(E, F // 128, 128).transpose(0, 2, 1)),
        b2t=np.ascontiguousarray(b2.reshape(E, D // 128, 128).transpose(0, 2, 1)),
        wo2=Wo2, bo=bo.reshape(1, D),
    )

    in_maps = []
    for c in range(8):
        b, hf = c // 2, c % 2
        if hf == 1:
            xp = X[b]
            cm = np.ones((1, NCH), dtype=np.float32)
        else:
            xp = np.concatenate([X[b, T // 2:], X[b, :T // 2]], axis=0)
            cm = np.ones((1, NCH), dtype=np.float32)
            cm[0, :NCH // 2] = 0.0
        m = dict(shared)
        m["xp"] = np.ascontiguousarray(xp)
        m["xr"] = np.ascontiguousarray(X[b])
        m["cmask"] = cm
        in_maps.append(m)

    if not _NC_CACHE:
        _NC_CACHE.append(build_nc())
    nc = _NC_CACHE[0]
    res = run_bass_kernel_spmd(nc, in_maps, core_ids=list(range(8)))

    outp = np.empty((B, T, D), dtype=np.float32)
    for c in range(8):
        b, hf = c // 2, c % 2
        outp[b, hf * (T // 2):(hf + 1) * (T // 2)] = res.results[c]["out"]
    return outp


# version 10 (final)
# speedup vs baseline: 1.1379x; 1.1014x over previous
import numpy as np
import concourse.bass as bass
import concourse.mybir as mybir
from concourse.bass_utils import run_bass_kernel_spmd

F32 = mybir.dt.float32
BF16 = mybir.dt.bfloat16
I32 = mybir.dt.int32
AF = mybir.ActivationFunctionType
ALU = mybir.AluOpType
AX = mybir.AxisListType

B, T, D = 4, 4096, 1024
H, FD, HD = 16, 16, 64
C = 128
NCH = T // C
F = 2048
E = 8
NAV = 128
EPS_LN = 1e-5
EPS_Z = 1e-10


class Sem:
    def __init__(self, h):
        self.h = h
        self.n = 0


def inc(ins, s, k=1):
    ins.then_inc(s.h, k)
    s.n += k
    return ins


def wge(eng, s, n=None):
    n = s.n if n is None else n
    assert n <= s.n, "wait on not-yet-emitted increment"
    if n > 0:
        eng.wait_ge(s.h, n)


class AD:
    def __init__(self, e):
        self._e = e

    def __getattr__(self, name):
        fn = getattr(self._e, name)
        if name in ("wait_ge", "drain") or not callable(fn):
            return fn
        def wrapped(*a, **k):
            self._e.drain()
            return fn(*a, **k)
        return wrapped


SM_GAMMA, SM_BETA, SM_BP1, SM_BP2 = 0, 16, 32, 160
SM_BO, SM_CM, SM_LG, SM_OH, SM_MX = 168, 1192, 1224, 1232, 1240
SM_L1, SM_G, SM_BT, SM_ONE1, SM_ONES = 1248, 1376, 1504, 2528, 2530
SMF_N = 2688
SB_OH, SB_BT = 0, 8
SMB_N = 1152

A_WQK, A_WV, A_WO = 0, 4096, 12288
A_XB, A_XT = 20480, 21504
A_XNQ, A_XNK, A_XNQT, A_XNKT = 22528, 23552, 24576, 25600
A_QF, A_KF, A_QFT = 26624, 30720, 34816
A_VSB, A_ATTM, A_OSB, A_OT, A_STB = 38912, 39952, 42000, 43024, 44048
A_W1E, A_W2E, A_HSB, A_X1T, A_RT, A_WO2 = 0, 16384, 32768, 40960, 45056, 49152
ARENA_N = 57344

AF_QK, AF_ZY, AF_ZZ, AF_X1R, AF_WP1, AF_XRT, AF_WP2 = 0, 512, 1024, 1664, 2688, 3712, 3728
AF_X2, AF_W2S = 0, 2048
ARENAF_N = 4096


def build_nc():
    nc = bass.Bass("TRN2", target_bir_lowering=False)
    d = {}
    for name, shape in [
        ("xp", [T, D]), ("xr", [T, D]), ("cmask", [1, NCH]),
        ("wqk", [D, 512]), ("wv", [D, D]), ("wo", [D, D]),
        ("gamma", [1, FD]), ("beta", [1, FD]),
        ("wp1", [D, NAV]), ("bp1", [1, NAV]), ("wp2", [NAV, E]), ("bp2", [1, E]),
        ("W1", [E, D, F]), ("W2", [E, F, D]),
        ("b1t", [E, 128, F // 128]), ("b2t", [E, 128, D // 128]),
        ("wo2", [D, D]), ("bo", [1, D]),
    ]:
        d[name] = nc.dram_tensor(name, shape, F32, kind="ExternalInput").ap()
    out_h = nc.dram_tensor("out", [T // 2, D], F32, kind="ExternalOutput").ap()

    from contextlib import ExitStack
    with ExitStack() as stack:
        en = stack.enter_context
        AR = en(nc.sbuf_tensor("arena", [128, ARENA_N], BF16))
        ARF = en(nc.sbuf_tensor("arf", [128, ARENAF_N], F32))
        xf = en(nc.sbuf_tensor("xf", [128, 2 * D], F32))
        cent = en(nc.sbuf_tensor("cent", [128, 1024], F32))
        rbuf = en(nc.sbuf_tensor("rbuf", [128, 1024], F32))
        stats = en(nc.sbuf_tensor("stats", [128, 16], F32))
        wstage = en(nc.sbuf_tensor("wstage", [128, 2 * 2048], F32))
        x1 = en(nc.sbuf_tensor("x1", [128, 16 * 1024], BF16))
        smf = en(nc.sbuf_tensor("smf", [1, SMF_N], F32))
        smb = en(nc.sbuf_tensor("smb", [1, SMB_N], BF16))
        ident = en(nc.sbuf_tensor("ident", [128, 128], BF16))
        identf = en(nc.sbuf_tensor("identf", [128, 128], F32))
        masku = en(nc.sbuf_tensor("masku", [128, 128], BF16))
        ones1 = en(nc.sbuf_tensor("ones1", [1, 128], BF16))
        ioti = en(nc.sbuf_tensor("ioti", [128, 128], I32))
        Ksa = en(nc.sbuf_tensor("Ksa", [128, 32], F32))
        Ksb = en(nc.sbuf_tensor("Ksb", [128, 32], BF16))
        onescol = en(nc.sbuf_tensor("onescol", [128, 1], BF16))
        gbr = en(nc.sbuf_tensor("gbr", [128, 32], F32))
        crep = en(nc.sbuf_tensor("crep", [128, 64], F32))
        mrep = en(nc.sbuf_tensor("mrep", [128, 8], F32))
        b1e = en(nc.sbuf_tensor("b1e", [128, 16], F32))
        b2e = en(nc.sbuf_tensor("b2e", [128, 8], F32))
        b2eb = en(nc.sbuf_tensor("b2eb", [128, 8], BF16))
        ps_state = en(nc.psum_tensor("ps_state", [128, 2048], F32))
        ps_m = en(nc.psum_tensor("ps_m", [128, 512], F32))
        ps_a = en(nc.psum_tensor("ps_a", [128, 512], F32))
        ps_b = en(nc.psum_tensor("ps_b", [128, 512], F32))
        ps_o = en(nc.psum_tensor("ps_o", [128, 512], F32))
        sems = {}
        for nm in ("sXa", "sXb", "sW0", "sW1", "sM", "sG0", "sG1",
                   "sO0", "sO1", "sT", "sS", "sV", "sI"):
            sems[nm] = Sem(en(nc.semaphore(nm)))
        blk = en(nc.Block())
        sXa, sXb, sW0, sW1, sM = (sems["sXa"], sems["sXb"], sems["sW0"],
                                  sems["sW1"], sems["sM"])
        sG0, sG1, sO0, sO1 = sems["sG0"], sems["sG1"], sems["sO0"], sems["sO1"]
        sT, sS, sV, sI = sems["sT"], sems["sS"], sems["sV"], sems["sI"]
        marks = {}
        started = set()

        def VEC(f):
            return blk.vector(lambda eng: f(AD(eng)))

        def SCA(f):
            return blk.scalar(f)

        def a(off, n):
            return AR[:, off:off + n]

        def af(off, n):
            return ARF[:, off:off + n]

        @blk.gpsimd
        def _(eng):
            inc(eng.iota(ioti[:], pattern=[[1, 128]], base=0, channel_multiplier=-1), sI)

        @VEC
        def _(eng):
            wge(eng, sI)
            eng.tensor_scalar(ident[:], ioti[:], 0, None, ALU.is_equal)
            eng.tensor_scalar(identf[:], ioti[:], 0, None, ALU.is_equal)
            eng.tensor_scalar(masku[:], ioti[:], 0, None, ALU.is_ge)
            eng.memset(ones1[:], 1.0)
            eng.memset(smf[0:1, SM_ONE1:SM_ONE1 + 1], 1.0)
            eng.memset(smf[0:1, SM_ONES:SM_ONES + 128], 1.0)
            eng.memset(a(A_XNQ, 1024), 0.0)
            eng.memset(a(A_XNK, 1024), 0.0)
            eng.memset(Ksa[:], 0.0)
            eng.memset(onescol[:], 1.0)
            inc(eng.memset(a(A_VSB, 1040), 1.0), sV)

        @blk.sync
        def _(eng):
            inc(eng.dma_start(smf[0:1, SM_GAMMA:SM_GAMMA + 16], d["gamma"]), sM, 16)
            inc(eng.dma_start(smf[0:1, SM_BETA:SM_BETA + 16], d["beta"]), sM, 16)
            inc(eng.dma_start(smf[0:1, SM_BP1:SM_BP1 + 128], d["bp1"]), sM, 16)
            inc(eng.dma_start(smf[0:1, SM_BP2:SM_BP2 + 8], d["bp2"]), sM, 16)
            inc(eng.dma_start(smf[0:1, SM_BO:SM_BO + 1024], d["bo"]), sM, 16)
            inc(eng.dma_start(smf[0:1, SM_CM:SM_CM + 32], d["cmask"]), sM, 16)
            for i in range(8):
                inc(eng.dma_start(af(AF_WP1 + i * 128, 128),
                                  d["wp1"].rearrange("(a p) n -> a p n", p=128)[i]), sM, 16)
            inc(eng.dma_start(af(AF_WP2, 8), d["wp2"].rearrange("p n -> p n")), sM, 16)
        mark_small = sM.n

        @blk.tensor
        def _(eng):
            wge(eng, sM, mark_small)
            wge(eng, sV)
            onesf = smf[0:1, SM_ONES:SM_ONES + 128]
            eng.matmul(ps_a[:, 0:16], onesf, smf[0:1, SM_GAMMA:SM_GAMMA + 16], start=True, stop=True)
            eng.matmul(ps_a[:, 16:32], onesf, smf[0:1, SM_BETA:SM_BETA + 16], start=True, stop=True)
            inc(eng.matmul(ps_a[:, 32:64], onesf, smf[0:1, SM_CM:SM_CM + 32], start=True, stop=True), sT)

        @SCA
        def _(eng):
            wge(eng, sT)
            eng.copy(gbr[:, 0:32], ps_a[:, 0:32])
            eng.copy(crep[:, 0:32], ps_a[:, 32:64])
            inc(eng.memzero(crep[:, 32:64]), sS)

        @VEC
        def _(eng):
            wge(eng, sS)
            inc(eng.tensor_scalar(crep[:, 32:64], crep[:, 32:64], 1.0, None, ALU.add), sV)
        mark_rep = sV.n

        wload = ([("wqk", A_WQK, 512, i) for i in range(8)] +
                 [("wv", A_WV, 1024, i) for i in range(8)] +
                 [("wo", A_WO, 1024, i) for i in range(8)] +
                 [("wo2", A_WO2, 1024, i) for i in range(8)])
        for j, (nm, base, ncols, i) in enumerate(wload):
            bufo = (j % 2) * 2048
            sw = sW0 if j % 2 == 0 else sW1
            @blk.sync
            def _(eng, nm=nm, i=i, bufo=bufo, ncols=ncols, j=j):
                mk = marks.get(("wst", j % 2))
                if mk is not None:
                    wge(eng, sS, mk)
                inc(eng.dma_start(wstage[:, bufo:bufo + ncols],
                                  d[nm].rearrange("(a p) n -> a p n", p=128)[i]), sw, 16)
            @SCA
            def _(eng, base=base, i=i, bufo=bufo, ncols=ncols, sw=sw):
                wge(eng, sw)
                inc(eng.copy(a(base + i * ncols, ncols), wstage[:, bufo:bufo + ncols]), sS)
            marks[("wst", j % 2)] = sS.n
        mark_p0cast = sS.n

        def kchunk(xin, ch, do_q, do_copies, mask_off, x1_dest, phase):
            bi = ch % 2
            sL = sXa if bi == 0 else sXb
            xfv = xf[:, bi * D:(bi + 1) * D]
            xb = a(A_XB, 1024)
            xt = a(A_XT, 1024)

            @blk.sync
            def _(eng):
                mk = marks.get(("xf", bi))
                if mk is not None:
                    wge(eng, mk[0], mk[1])
                inc(eng.dma_start(xfv, xin[ch * 128:(ch + 1) * 128, :]), sL, 16)

            @SCA
            def _(eng):
                wge(eng, sL)
                wge(eng, sT)
                inc(eng.copy(xb, xfv), sS)

            @blk.tensor
            def _(eng):
                wge(eng, sS)
                wge(eng, sV)
                for k in range(8):
                    ps = ps_a if k < 4 else ps_b
                    ins = eng.matmul(ps[:, (k % 4) * 128:(k % 4 + 1) * 128],
                                     xb[:, k * 128:(k + 1) * 128], ident[:],
                                     start=True, stop=True)
                inc(ins, sT)

            @SCA
            def _(eng):
                wge(eng, sT)
                eng.copy(xt[:, 0:512], ps_a[:])
                inc(eng.copy(xt[:, 512:1024], ps_b[:]), sS)

            @blk.tensor
            def _(eng):
                wge(eng, sS)
                for dsl in range(8):
                    eng.matmul(ps_a[:], xt[:, dsl * 128:(dsl + 1) * 128],
                               a(A_WQK + dsl * 512, 512), start=(dsl == 0), stop=(dsl == 7))
                for dsl in range(8):
                    eng.matmul(ps_b[:], xt[:, dsl * 128:(dsl + 1) * 128],
                               a(A_WV + dsl * 1024, 512), start=(dsl == 0), stop=(dsl == 7))
                for dsl in range(8):
                    ins = eng.matmul(ps_o[:], xt[:, dsl * 128:(dsl + 1) * 128],
                                     a(A_WV + dsl * 1024 + 512, 512), start=(dsl == 0), stop=(dsl == 7))
                inc(ins, sT)

            @SCA
            def _(eng):
                wge(eng, sT)
                eng.copy(af(AF_QK, 512), ps_a[:])
                v3 = a(A_VSB, 1040).rearrange("p (h c) -> p h c", c=65)
                eng.copy(v3[:, 0:8, 0:64], ps_b[:].rearrange("p (h c) -> p h c", c=64))
                inc(eng.copy(v3[:, 8:16, 0:64], ps_o[:].rearrange("p (h c) -> p h c", c=64)), sS)
            if not do_q:
                marks[("xf", bi)] = (sS, sS.n)

            zzv = af(AF_ZZ, 640)
            @VEC
            def _(eng):
                wge(eng, sS)
                q3 = af(AF_QK, 512).rearrange("p (g f) -> p g f", f=16)
                mu = zzv[:, 512:544].rearrange("p (g o) -> p g o", o=1)
                eng.reduce_sum(mu, q3, axis=AX.X)
                eng.tensor_scalar(zzv[:, 512:544], zzv[:, 512:544], 1.0 / 16, None, ALU.mult)
                c3 = cent[:, 0:512].rearrange("p (g f) -> p g f", f=16)
                eng.tensor_tensor(c3, q3, mu.broadcast_to([128, 32, 16]), ALU.subtract)
                s3 = cent[:, 512:1024].rearrange("p (g f) -> p g f", f=16)
                eng.tensor_tensor(s3, c3, c3, ALU.mult)
                vs = zzv[:, 544:576].rearrange("p (g o) -> p g o", o=1)
                eng.reduce_sum(vs, s3, axis=AX.X)
                inc(eng.tensor_scalar(zzv[:, 544:576], zzv[:, 544:576],
                                      1.0 / 16, EPS_LN, ALU.mult, ALU.add), sV)

            @SCA
            def _(eng):
                wge(eng, sV)
                inc(eng.activation(zzv[:, 608:640], zzv[:, 544:576], AF.Sqrt), sS)

            @VEC
            def _(eng):
                wge(eng, sS)
                eng.reciprocal(zzv[:, 576:608], zzv[:, 608:640])
                c3 = cent[:, 0:512].rearrange("p (g f) -> p g f", f=16)
                rs = zzv[:, 576:608].rearrange("p (g o) -> p g o", o=1)
                eng.tensor_tensor(c3, c3, rs.broadcast_to([128, 32, 16]), ALU.mult)
                gam = gbr[:, 0:16].unsqueeze(1).broadcast_to([128, 16, 16])
                bet = gbr[:, 16:32].unsqueeze(1).broadcast_to([128, 16, 16])
                xk3 = a(A_XNK, 1024).rearrange("p (h f) -> p h f", f=64)[:, :, 0:16]
                ck = cent[:, 256:512].rearrange("p (h f) -> p h f", f=16)
                eng.tensor_tensor(xk3, ck, gam, ALU.mult)
                eng.tensor_tensor(xk3, xk3, bet, ALU.add)
                mk = crep[:, mask_off + ch:mask_off + ch + 1]
                eng.tensor_scalar(xk3, xk3, mk, None, ALU.mult)
                kk = a(A_XNK, 1024).rearrange("p (h f) -> p h f", f=64)[:, :, 0:16]
                eng.tensor_tensor(a(A_KF, 4096).rearrange("p (h i j) -> p h i j", i=16, j=16),
                                  kk.unsqueeze(3).broadcast_to([128, 16, 16, 16]),
                                  kk.unsqueeze(2).broadcast_to([128, 16, 16, 16]), ALU.mult)
                if do_q:
                    xq3 = a(A_XNQ, 1024).rearrange("p (h f) -> p h f", f=64)[:, :, 0:16]
                    cq = cent[:, 0:256].rearrange("p (h f) -> p h f", f=16)
                    eng.tensor_tensor(xq3, cq, gam, ALU.mult)
                    eng.tensor_tensor(xq3, xq3, bet, ALU.add)
                    qq = a(A_XNQ, 1024).rearrange("p (h f) -> p h f", f=64)[:, :, 0:16]
                    eng.tensor_tensor(a(A_QF, 4096).rearrange("p (h i j) -> p h i j", i=16, j=16),
                                      qq.unsqueeze(3).broadcast_to([128, 16, 16, 16]),
                                      qq.unsqueeze(2).broadcast_to([128, 16, 16, 16]), ALU.mult)
                inc(eng.engine_nop(), sV)

            if do_copies:
                @SCA
                def _(eng):
                    wge(eng, sT)
                    wge(eng, sV)
                    eng.activation(a(A_STB, 2048), ps_state[:], AF.Copy, bias=0.0, scale=1.0 / 16)
                    inc(eng.activation(Ksb[:], Ksa[:], AF.Copy, bias=0.0, scale=1.0 / 16), sS)
            mark_copies = sS.n

            if do_q:
                bk4 = [ps_a, ps_b, ps_o, ps_m]
                @blk.tensor
                def _(eng):
                    wge(eng, sS)
                    wge(eng, sV)
                    for k in range(16):
                        srcb = A_XNQ if k < 8 else A_XNK - 1024
                        ps = bk4[k // 4]
                        ins = eng.matmul(ps[:, (k % 4) * 128:(k % 4 + 1) * 128],
                                         a(srcb + k * 128, 128), ident[:], start=True, stop=True)
                    inc(ins, sT)
                @SCA
                def _(eng):
                    wge(eng, sT)
                    eng.copy(a(A_XNQT, 512), ps_a[:])
                    eng.copy(a(A_XNQT + 512, 512), ps_b[:])
                    eng.copy(a(A_XNKT, 512), ps_o[:])
                    inc(eng.copy(a(A_XNKT + 512, 512), ps_m[:]), sS)
                for wv_ in range(2):
                    @blk.tensor
                    def _(eng, wv_=wv_):
                        wge(eng, sS)
                        for k in range(16):
                            ps = bk4[k // 4]
                            ins = eng.matmul(ps[:, (k % 4) * 128:(k % 4 + 1) * 128],
                                             a(A_QF + wv_ * 2048 + k * 128, 128), ident[:],
                                             start=True, stop=True)
                        inc(ins, sT)
                    @SCA
                    def _(eng, wv_=wv_):
                        wge(eng, sT)
                        eng.copy(a(A_QFT + wv_ * 2048, 512), ps_a[:])
                        eng.copy(a(A_QFT + wv_ * 2048 + 512, 512), ps_b[:])
                        eng.copy(a(A_QFT + wv_ * 2048 + 1024, 512), ps_o[:])
                        inc(eng.copy(a(A_QFT + wv_ * 2048 + 1536, 512), ps_m[:]), sS)

            @blk.tensor
            def _(eng):
                wge(eng, sS)
                wge(eng, sV)
                for h in range(16):
                    for kc in range(2):
                        col = (h * 2 + kc) * 64
                        key = (phase, "st", col // 512)
                        st = key not in started
                        started.add(key)
                        eng.matmul(ps_state[:, col:col + 64],
                                   a(A_KF + h * 256 + kc * 128, 128),
                                   a(A_VSB + h * 65, 64), start=st, stop=False,
                                   skip_group_check=True)
                ins = None
                for h in range(16):
                    for kc in range(2):
                        col = h * 2 + kc
                        ins = eng.matmul(ps_m[:, col:col + 1],
                                         a(A_KF + h * 256 + kc * 128, 128),
                                         onescol[:], start=True, stop=True)
                inc(ins, sT)

            @VEC
            def _(eng):
                wge(eng, sT)
                inc(eng.tensor_tensor(Ksa[:], Ksa[:], ps_m[:, 0:32], ALU.add), sV)

            if not do_q:
                return

            xnqT = a(A_XNQT, 1024)
            xnkT = a(A_XNKT, 1024)
            banks4 = [ps_a, ps_b, ps_o, ps_m]
            marks.pop("attg", None)
            marks.pop("og", None)
            for g in range(4):
                @blk.tensor
                def _(eng, g=g):
                    mk = marks.get("attg")
                    if mk is not None:
                        wge(eng, sS, mk)
                    else:
                        wge(eng, sS)
                    wge(eng, sV)
                    for hh in range(4):
                        h = g * 4 + hh
                        off = 64 * (h % 2)
                        tl = (h // 2) * 128
                        ins = eng.matmul(banks4[hh][:, 0:128],
                                         xnkT[off:off + 64, tl:tl + 128],
                                         xnqT[off:off + 64, tl:tl + 128], start=True, stop=True)
                    inc(ins, sT)
                @SCA
                def _(eng, g=g):
                    wge(eng, sT)
                    for hh in range(4):
                        h = g * 4 + hh
                        ins = eng.activation(a(A_ATTM + h * 128, 128), banks4[hh][:, 0:128],
                                             AF.Square, bias=0.0, scale=0.25)
                    inc(ins, sS)
                marks["attg"] = sS.n

            @VEC
            def _(eng):
                wge(eng, sS)
                am = a(A_ATTM, 2048).rearrange("p (h t) -> p h t", t=128)
                inc(eng.tensor_tensor(am, am, masku[:].unsqueeze(1).broadcast_to([128, 16, 128]),
                                      ALU.mult), sV)

            for g in range(4):
                @blk.tensor
                def _(eng, g=g):
                    mk = marks.get("og")
                    if mk is not None:
                        wge(eng, sS, mk)
                    else:
                        wge(eng, sS)
                    wge(eng, sV)
                    for hh in range(4):
                        h = g * 4 + hh
                        bank = banks4[hh]
                        eng.matmul(bank[:, 0:65], a(A_ATTM + h * 128, 128),
                                   a(A_VSB + h * 65, 65), start=True, stop=False)
                        for kc in range(2):
                            eng.matmul(bank[:, 0:64], a(A_QFT + h * 256 + kc * 128, 128),
                                       a(A_STB + (h * 2 + kc) * 64, 64),
                                       start=False, stop=False)
                        for kc in range(2):
                            ins = eng.matmul(bank[:, 64:65], a(A_QFT + h * 256 + kc * 128, 128),
                                             Ksb[:, h * 2 + kc:h * 2 + kc + 1],
                                             start=False, stop=(kc == 1))
                    inc(ins, sT)
                @SCA
                def _(eng, g=g):
                    wge(eng, sT)
                    for hh in range(4):
                        h = g * 4 + hh
                        eng.copy(a(A_OSB + h * 64, 64), banks4[hh][:, 0:64])
                        ins = eng.copy(af(AF_ZY + h, 1), banks4[hh][:, 64:65])
                    inc(ins, sS)
                marks["og"] = sS.n

            @VEC
            def _(eng):
                wge(eng, sS)
                eng.tensor_scalar(zzv[:, 16:32], af(AF_ZY, 16), EPS_Z, None, ALU.add)
                eng.reciprocal(zzv[:, 32:48], zzv[:, 16:32])
                o3 = a(A_OSB, 1024).rearrange("p (h c) -> p h c", c=64)
                zr = zzv[:, 32:48].rearrange("p (h o) -> p h o", o=1)
                inc(eng.tensor_tensor(o3, o3, zr.broadcast_to([128, 16, 64]), ALU.mult), sV)

            @blk.tensor
            def _(eng):
                wge(eng, sV)
                wge(eng, sS)
                for k in range(8):
                    ps = ps_a if k < 4 else ps_b
                    ins = eng.matmul(ps[:, (k % 4) * 128:(k % 4 + 1) * 128],
                                     a(A_OSB + k * 128, 128), ident[:], start=True, stop=True)
                inc(ins, sT)
            @SCA
            def _(eng):
                wge(eng, sT)
                eng.copy(a(A_OT, 512), ps_a[:])
                inc(eng.copy(a(A_OT + 512, 512), ps_b[:]), sS)
            @blk.tensor
            def _(eng):
                wge(eng, sS)
                for bk in range(8):
                    for w in range(2):
                        ps = ps_a if w == 0 else ps_b
                        ins = eng.matmul(ps[:], a(A_OT + bk * 128, 128),
                                         a(A_WO + bk * 1024 + w * 512, 512),
                                         start=(bk == 0), stop=(bk == 7))
                inc(ins, sT)

            @VEC
            def _(eng):
                wge(eng, sT)
                eng.tensor_tensor(rbuf[:, 0:512], ps_a[:], xfv[:, 0:512], ALU.add)
                eng.tensor_tensor(rbuf[:, 512:1024], ps_b[:], xfv[:, 512:1024], ALU.add)
                eng.reduce_sum(stats[:, 0:1], rbuf[:], axis=AX.X)
                eng.tensor_tensor(cent[:], rbuf[:], rbuf[:], ALU.mult)
                eng.reduce_sum(stats[:, 1:2], cent[:], axis=AX.X)
                eng.tensor_scalar(stats[:, 2:3], stats[:, 0:1], 1.0 / D, None, ALU.mult)
                eng.tensor_tensor(stats[:, 3:4], stats[:, 2:3], stats[:, 2:3], ALU.mult)
                eng.scalar_tensor_tensor(stats[:, 4:5], stats[:, 1:2], 1.0 / D, stats[:, 3:4],
                                         ALU.mult, ALU.subtract)
                inc(eng.tensor_scalar(stats[:, 4:5], stats[:, 4:5], EPS_LN, None, ALU.add), sV)
            @SCA
            def _(eng):
                wge(eng, sV)
                inc(eng.activation(stats[:, 7:8], stats[:, 4:5], AF.Sqrt), sS)
            @VEC
            def _(eng):
                wge(eng, sS)
                eng.reciprocal(stats[:, 5:6], stats[:, 7:8])
                eng.tensor_tensor(stats[:, 6:7], stats[:, 2:3], stats[:, 5:6], ALU.mult)
                eng.tensor_scalar(stats[:, 6:7], stats[:, 6:7], -1.0, None, ALU.mult)
                dst, dco = x1_dest
                inc(eng.scalar_tensor_tensor(dst[:, dco:dco + 1024], rbuf[:], stats[:, 5:6],
                                             stats[:, 6:7].broadcast_to([128, 1024]),
                                             ALU.mult, ALU.add), sV)
            marks[("xf", bi)] = (sV, sV.n)

        x1r = af(AF_X1R, 1024)
        for ch in range(NCH):
            kchunk(d["xr"], ch, do_q=(ch == NCH - 1), do_copies=(ch == NCH - 1),
                   mask_off=32, x1_dest=(ARF, AF_X1R), phase=1)

        xrt = af(AF_XRT, 16)
        @blk.tensor
        def _(eng):
            wge(eng, sV)
            wge(eng, sS)
            for k in range(8):
                ps = ps_a if k < 4 else ps_b
                ins = eng.matmul(ps[:, (k % 4) * 128:(k % 4 + 1) * 128],
                                 x1r[:, k * 128:(k + 1) * 128], identf[:], start=True, stop=True)
            inc(ins, sT)
        @SCA
        def _(eng):
            wge(eng, sT)
            ins = None
            for k in range(8):
                ps = ps_a if k < 4 else ps_b
                ins = eng.copy(xrt[:, k:k + 1], ps[:, (k % 4) * 128 + 127:(k % 4) * 128 + 128])
            inc(ins, sS)
        @blk.tensor
        def _(eng):
            wge(eng, sS)
            for k in range(8):
                ins = eng.matmul(ps_o[0:1, 0:128], xrt[:, k:k + 1],
                                 af(AF_WP1 + k * 128, 128), start=(k == 0), stop=(k == 7))
            inc(ins, sT)
        @VEC
        def _(eng):
            wge(eng, sT)
            inc(eng.tensor_tensor(smf[0:1, SM_L1:SM_L1 + 128], ps_o[0:1, 0:128],
                                  smf[0:1, SM_BP1:SM_BP1 + 128], ALU.add), sV)
        @SCA
        def _(eng):
            wge(eng, sV)
            inc(eng.activation(smf[0:1, SM_G:SM_G + 128], smf[0:1, SM_L1:SM_L1 + 128], AF.Gelu), sS)
        @blk.tensor
        def _(eng):
            wge(eng, sS)
            inc(eng.matmul(ps_o[:, 128:129], smf[0:1, SM_G:SM_G + 128],
                           smf[0:1, SM_ONE1:SM_ONE1 + 1], start=True, stop=True), sT)
        @SCA
        def _(eng):
            wge(eng, sT)
            inc(eng.copy(xrt[:, 8:9], ps_o[:, 128:129]), sS)
        @blk.tensor
        def _(eng):
            wge(eng, sS)
            inc(eng.matmul(ps_o[0:1, 136:144], xrt[:, 8:9], af(AF_WP2, 8), start=True, stop=True), sT)
        @VEC
        def _(eng):
            wge(eng, sT)
            eng.tensor_tensor(smf[0:1, SM_LG:SM_LG + 8], ps_o[0:1, 136:144],
                              smf[0:1, SM_BP2:SM_BP2 + 8], ALU.add)
            eng.reduce_max(smf[0:1, SM_MX:SM_MX + 1], smf[0:1, SM_LG:SM_LG + 8], axis=AX.X)
            inc(eng.tensor_tensor(smf[0:1, SM_OH:SM_OH + 8], smf[0:1, SM_LG:SM_LG + 8],
                                  smf[0:1, SM_MX:SM_MX + 1].broadcast_to([1, 8]), ALU.is_equal), sV)
        @SCA
        def _(eng):
            wge(eng, sV)
            inc(eng.copy(smb[0:1, SB_OH:SB_OH + 8], smf[0:1, SM_OH:SM_OH + 8]), sS)
        @blk.tensor
        def _(eng):
            wge(eng, sS)
            inc(eng.matmul(ps_o[:, 144:152], ones1[:], smb[0:1, SB_OH:SB_OH + 8],
                           start=True, stop=True), sT)
        @SCA
        def _(eng):
            wge(eng, sT)
            inc(eng.copy(mrep[:], ps_o[:, 144:152]), sS)
        mark_mrep = sS.n

        @blk.sync
        def _(eng):
            wge(eng, sT)
            wge(eng, sV)
            inc(eng.dma_start(af(AF_W2S, 128).rearrange("p (e f) -> p e f", e=8),
                              d["b1t"].rearrange("e p f -> p e f")), sM, 16)
            inc(eng.dma_start(af(AF_W2S + 128, 64).rearrange("p (e f) -> p e f", e=8),
                              d["b2t"].rearrange("e p f -> p e f")), sM, 16)
        @VEC
        def _(eng):
            wge(eng, sM)
            wge(eng, sS, mark_mrep)
            b13 = af(AF_W2S, 128).rearrange("p (e f) -> p e f", e=8)
            b23 = af(AF_W2S + 128, 64).rearrange("p (e f) -> p e f", e=8)
            for e in range(E):
                if e == 0:
                    eng.tensor_scalar(b1e[:], b13[:, 0, :], mrep[:, 0:1], None, ALU.mult)
                    eng.tensor_scalar(b2e[:], b23[:, 0, :], mrep[:, 0:1], None, ALU.mult)
                else:
                    eng.scalar_tensor_tensor(b1e[:], b13[:, e, :], mrep[:, e:e + 1], b1e[:],
                                             ALU.mult, ALU.add)
                    eng.scalar_tensor_tensor(b2e[:], b23[:, e, :], mrep[:, e:e + 1], b2e[:],
                                             ALU.mult, ALU.add)
            inc(eng.engine_nop(), sV)
        @SCA
        def _(eng):
            wge(eng, sV)
            inc(eng.copy(b2eb[:], b2e[:]), sS)
        @blk.tensor
        def _(eng):
            wge(eng, sS)
            for w in range(2):
                for dsl in range(8):
                    ins = eng.matmul((ps_a if w == 0 else ps_b)[0:1, 0:512],
                                     b2eb[:, dsl:dsl + 1],
                                     a(A_WO2 + dsl * 1024 + w * 512, 512),
                                     start=(dsl == 0), stop=(dsl == 7))
            inc(ins, sT)
        mark_router_t = sT.n
        @VEC
        def _(eng):
            wge(eng, sT)
            eng.tensor_tensor(smf[0:1, SM_BT:SM_BT + 512], ps_a[0:1, 0:512],
                              smf[0:1, SM_BO:SM_BO + 512], ALU.add)
            inc(eng.tensor_tensor(smf[0:1, SM_BT + 512:SM_BT + 1024], ps_b[0:1, 0:512],
                                  smf[0:1, SM_BO + 512:SM_BO + 1024], ALU.add), sV)
        @SCA
        def _(eng):
            wge(eng, sV)
            inc(eng.copy(smb[0:1, SB_BT:SB_BT + 1024], smf[0:1, SM_BT:SM_BT + 1024]), sS)

        @VEC
        def _(eng):
            inc(eng.memset(Ksa[:], 0.0), sV)

        for ch in range(NCH):
            kchunk(d["xp"], ch, do_q=(ch >= 16), do_copies=(ch >= 16),
                   mask_off=0, x1_dest=(x1, (ch - 16) * 1024), phase=2)

        mark_w1 = {}
        for dsl in range(8):
            for e in range(E):
                j = dsl * 8 + e
                bufo = (j % 2) * 2048
                sw = sW0 if j % 2 == 0 else sW1
                @blk.sync
                def _(eng, dsl=dsl, e=e, bufo=bufo, j=j, sw=sw):
                    mk = marks.get(("w1s", j % 2))
                    if mk is not None:
                        wge(eng, sV, mk)
                    else:
                        wge(eng, sS, mark_p0cast)
                    inc(eng.dma_start(wstage[:, bufo:bufo + 2048],
                                      d["W1"][e].rearrange("(a p) f -> a p f", p=128)[dsl]), sw, 16)
                @VEC
                def _(eng, dsl=dsl, e=e, bufo=bufo, sw=sw):
                    wge(eng, sw)
                    if e == 0:
                        wge(eng, sS, mark_mrep)
                        ins = eng.tensor_scalar(a(A_W1E + dsl * 2048, 2048),
                                                wstage[:, bufo:bufo + 2048],
                                                mrep[:, 0:1], None, ALU.mult)
                    else:
                        ins = eng.scalar_tensor_tensor(a(A_W1E + dsl * 2048, 2048),
                                                       wstage[:, bufo:bufo + 2048],
                                                       mrep[:, e:e + 1], a(A_W1E + dsl * 2048, 2048),
                                                       ALU.mult, ALU.add)
                    inc(ins, sV)
                marks[("w1s", j % 2)] = sV.n
        mark_w1done = sV.n

        for fsl in range(16):
            for e in range(E):
                j = fsl * 8 + e
                bufo = (j % 2) * 1024
                sg = sG0 if j % 2 == 0 else sG1
                @blk.gpsimd
                def _(eng, fsl=fsl, e=e, bufo=bufo, j=j, sg=sg):
                    mk = marks.get(("w2s", j % 2))
                    if mk is not None:
                        wge(eng, sV, mk)
                    else:
                        wge(eng, sT, mark_router_t)
                    inc(eng.dma_start(af(AF_W2S + bufo, 1024),
                                      d["W2"][e].rearrange("(a p) f -> a p f", p=128)[fsl]), sg, 16)
                @VEC
                def _(eng, fsl=fsl, e=e, bufo=bufo, sg=sg):
                    wge(eng, sg)
                    if e == 0:
                        ins = eng.tensor_scalar(a(A_W2E + fsl * 1024, 1024),
                                                af(AF_W2S + bufo, 1024),
                                                mrep[:, 0:1], None, ALU.mult)
                    else:
                        ins = eng.scalar_tensor_tensor(a(A_W2E + fsl * 1024, 1024),
                                                       af(AF_W2S + bufo, 1024),
                                                       mrep[:, e:e + 1], a(A_W2E + fsl * 1024, 1024),
                                                       ALU.mult, ALU.add)
                    inc(ins, sV)
                marks[("w2s", j % 2)] = sV.n
        mark_w2done = sV.n

        for stile in range(4):
            for key in (("hev", 0), ("hev", 1), ("rev", 0), ("rev", 1)):
                marks.pop(key, None)
            for tc in range(4):
                slot = stile * 4 + tc
                @blk.tensor
                def _(eng, slot=slot, tc=tc, stile=stile):
                    wge(eng, sS)
                    if stile == 0 and tc == 0:
                        wge(eng, sV, mark_w1done)
                    for k in range(8):
                        ps = ps_a if k < 4 else ps_b
                        ins = eng.matmul(ps[:, (k % 4) * 128:(k % 4 + 1) * 128],
                                         x1[:, slot * 1024 + k * 128: slot * 1024 + (k + 1) * 128],
                                         ident[:], start=True, stop=True)
                    inc(ins, sT)
                @SCA
                def _(eng, tc=tc):
                    wge(eng, sT)
                    x1t3 = a(A_X1T, 4096).rearrange("p (s t) -> p s t", t=512)
                    eng.copy(x1t3[:, 0:4, tc * 128:(tc + 1) * 128],
                             ps_a[:].rearrange("p (s t) -> p s t", t=128))
                    inc(eng.copy(x1t3[:, 4:8, tc * 128:(tc + 1) * 128],
                                 ps_b[:].rearrange("p (s t) -> p s t", t=128)), sS)

            for ft in range(16):
                bank = ps_a if ft % 2 == 0 else ps_b
                @blk.tensor
                def _(eng, ft=ft, bank=bank):
                    mk = marks.get(("hev", ft % 2))
                    if mk is not None:
                        wge(eng, sS, mk)
                    else:
                        wge(eng, sS)
                    for dsl in range(8):
                        ins = eng.matmul(bank[:], a(A_W1E + dsl * 2048 + ft * 128, 128),
                                         a(A_X1T + dsl * 512, 512), start=(dsl == 0), stop=(dsl == 7))
                    inc(ins, sT)
                @SCA
                def _(eng, ft=ft, bank=bank):
                    wge(eng, sT)
                    inc(eng.activation(a(A_HSB + ft * 512, 512), bank[:], AF.Gelu,
                                       bias=b1e[:, ft:ft + 1], scale=1.0), sS)
                marks[("hev", ft % 2)] = sS.n

            for dt in range(8):
                bank = ps_o if dt % 2 == 0 else ps_m
                @blk.tensor
                def _(eng, dt=dt, bank=bank):
                    mk = marks.get(("rev", dt % 2))
                    if mk is not None:
                        wge(eng, sS, mk)
                    else:
                        wge(eng, sS)
                    if dt == 0:
                        wge(eng, sV, mark_w2done)
                    for fsl in range(16):
                        ins = eng.matmul(bank[:], a(A_W2E + fsl * 1024 + dt * 128, 128),
                                         a(A_HSB + fsl * 512, 512), start=(fsl == 0), stop=(fsl == 15))
                    inc(ins, sT)
                @SCA
                def _(eng, dt=dt, bank=bank):
                    wge(eng, sT)
                    inc(eng.copy(a(A_RT + dt * 512, 512), bank[:]), sS)
                marks[("rev", dt % 2)] = sS.n

            for tc in range(4):
                slot = stile * 4 + tc
                pso0 = ps_state[:, (tc % 2) * 1024:(tc % 2) * 1024 + 512]
                pso1 = ps_state[:, (tc % 2) * 1024 + 512:(tc % 2) * 1024 + 1024]
                @blk.tensor
                def _(eng, tc=tc, pso0=pso0, pso1=pso1):
                    mk = marks.get(("o2", tc % 2))
                    if mk is not None:
                        wge(eng, sV, mk)
                    wge(eng, sS)
                    for w, pw in ((0, pso0), (1, pso1)):
                        for dsl in range(8):
                            eng.matmul(pw, a(A_RT + dsl * 512 + tc * 128, 128),
                                       a(A_WO2 + dsl * 1024 + w * 512, 512),
                                       start=(dsl == 0), stop=False)
                        ins = eng.matmul(pw, ones1[:], smb[0:1, SB_BT + w * 512:SB_BT + (w + 1) * 512],
                                         start=False, stop=True)
                    inc(ins, sT)
                @VEC
                def _(eng, slot=slot, pso0=pso0, pso1=pso1):
                    wge(eng, sT)
                    eng.scalar_tensor_tensor(rbuf[:, 0:512], pso0, 0.15,
                                             x1[:, slot * 1024:slot * 1024 + 512], ALU.mult, ALU.add)
                    eng.scalar_tensor_tensor(rbuf[:, 512:1024], pso1, 0.15,
                                             x1[:, slot * 1024 + 512:slot * 1024 + 1024],
                                             ALU.mult, ALU.add)
                    eng.reduce_sum(stats[:, 0:1], rbuf[:], axis=AX.X)
                    eng.tensor_tensor(cent[:], rbuf[:], rbuf[:], ALU.mult)
                    eng.reduce_sum(stats[:, 1:2], cent[:], axis=AX.X)
                    eng.tensor_scalar(stats[:, 2:3], stats[:, 0:1], 1.0 / D, None, ALU.mult)
                    eng.tensor_tensor(stats[:, 3:4], stats[:, 2:3], stats[:, 2:3], ALU.mult)
                    eng.scalar_tensor_tensor(stats[:, 4:5], stats[:, 1:2], 1.0 / D, stats[:, 3:4],
                                             ALU.mult, ALU.subtract)
                    inc(eng.tensor_scalar(stats[:, 4:5], stats[:, 4:5], EPS_LN / 4, None, ALU.add), sV)
                @SCA
                def _(eng):
                    wge(eng, sV)
                    inc(eng.activation(stats[:, 7:8], stats[:, 4:5], AF.Sqrt), sS)
                so = sO0 if slot % 2 == 0 else sO1
                @VEC
                def _(eng, slot=slot, so=so):
                    wge(eng, sS)
                    mko = marks.get(("x2", slot % 2))
                    if mko is not None:
                        wge(eng, so, mko)
                    eng.reciprocal(stats[:, 5:6], stats[:, 7:8])
                    eng.tensor_tensor(stats[:, 6:7], stats[:, 2:3], stats[:, 5:6], ALU.mult)
                    eng.tensor_scalar(stats[:, 6:7], stats[:, 6:7], -1.0, None, ALU.mult)
                    inc(eng.scalar_tensor_tensor(af(AF_X2 + (slot % 2) * 1024, 1024), rbuf[:],
                                                 stats[:, 5:6],
                                                 stats[:, 6:7].broadcast_to([128, 1024]),
                                                 ALU.mult, ALU.add), sV)
                marks[("o2", tc % 2)] = sV.n
                @blk.gpsimd
                def _(eng, slot=slot, so=so):
                    wge(eng, sV)
                    inc(eng.dma_start(out_h[slot * 128:(slot + 1) * 128, :],
                                      af(AF_X2 + (slot % 2) * 1024, 1024)), so, 16)
                marks[("x2", slot % 2)] = so.n

        @blk.gpsimd
        def _(eng):
            wge(eng, sO0, 8 * 16)
            wge(eng, sO1, 8 * 16)

    return nc


_NC_CACHE = []


def kernel(**inputs):
    X = np.asarray(inputs["L0_attn_hidden_states"], dtype=np.float32)
    Wq = np.asarray(inputs["L0_attn_Wq"], dtype=np.float32)
    Wk = np.asarray(inputs["L0_attn_Wk"], dtype=np.float32)
    Wv = np.asarray(inputs["L0_attn_Wv"], dtype=np.float32)
    Wo = np.asarray(inputs["L0_attn_Wo"], dtype=np.float32)
    gamma = np.asarray(inputs["L0_attn_gamma"], dtype=np.float32)
    beta = np.asarray(inputs["L0_attn_beta"], dtype=np.float32)
    Wp1 = np.asarray(inputs["L0_ffn_Wp1"], dtype=np.float32)
    bp1 = np.asarray(inputs["L0_ffn_bp1"], dtype=np.float32)
    Wp2 = np.asarray(inputs["L0_ffn_Wp2"], dtype=np.float32)
    bp2 = np.asarray(inputs["L0_ffn_bp2"], dtype=np.float32)
    W1 = np.asarray(inputs["L0_ffn_W1"], dtype=np.float32)
    b1 = np.asarray(inputs["L0_ffn_b1"], dtype=np.float32)
    W2 = np.asarray(inputs["L0_ffn_W2"], dtype=np.float32)
    b2 = np.asarray(inputs["L0_ffn_b2"], dtype=np.float32)
    Wo2 = np.asarray(inputs["L0_ffn_Wo"], dtype=np.float32)
    bo = np.asarray(inputs["L0_ffn_bo"], dtype=np.float32)

    shared = dict(
        wqk=np.ascontiguousarray(np.concatenate([Wq, Wk], axis=0).T),
        wv=np.ascontiguousarray(Wv.T),
        wo=np.ascontiguousarray(Wo.T),
        gamma=gamma.reshape(1, FD), beta=beta.reshape(1, FD),
        wp1=Wp1, bp1=bp1.reshape(1, NAV), wp2=Wp2, bp2=bp2.reshape(1, E),
        W1=W1, W2=W2,
        b1t=np.ascontiguousarray(b1.reshape(E, F // 128, 128).transpose(0, 2, 1)),
        b2t=np.ascontiguousarray(b2.reshape(E, D // 128, 128).transpose(0, 2, 1)),
        wo2=Wo2, bo=bo.reshape(1, D),
    )

    in_maps = []
    for c in range(8):
        b, hf = c // 2, c % 2
        if hf == 1:
            xp = X[b]
            cm = np.ones((1, NCH), dtype=np.float32)
        else:
            xp = np.concatenate([X[b, T // 2:], X[b, :T // 2]], axis=0)
            cm = np.ones((1, NCH), dtype=np.float32)
            cm[0, :NCH // 2] = 0.0
        m = dict(shared)
        m["xp"] = np.ascontiguousarray(xp)
        m["xr"] = np.ascontiguousarray(X[b])
        m["cmask"] = cm
        in_maps.append(m)

    if not _NC_CACHE:
        _NC_CACHE.append(build_nc())
    nc = _NC_CACHE[0]
    res = run_bass_kernel_spmd(nc, in_maps, core_ids=list(range(8)))

    outp = np.empty((B, T, D), dtype=np.float32)
    for c in range(8):
        b, hf = c // 2, c % 2
        outp[b, hf * (T // 2):(hf + 1) * (T // 2)] = res.results[c]["out"]
    return outp
